# version 12
# baseline (speedup 1.0000x reference)
import numpy as np
import contextlib
import concourse.bass as bass
import concourse.mybir as mybir
from concourse.bass_utils import run_bass_kernel_spmd

F32 = mybir.dt.float32
BF16 = mybir.dt.bfloat16
I32 = mybir.dt.int32
AF = mybir.ActivationFunctionType
ALU = mybir.AluOpType
AX = mybir.AxisListType

NCORE = 8
T = 2048
D = 1024
NT = T // 128
ND = D // 128


class Rec:
    def __init__(self, name):
        self.name = name
        self.q = []

    def op(self, meth, *a, **k):
        item = [meth, a, k, []]
        self.q.append(item)
        class H:
            def then_inc(self, sem, v=1):
                item[3].append((sem, v))
                return self
        return H()

    def __getattr__(self, meth):
        if meth.startswith("_") or meth in ("q", "name", "op", "replay"):
            raise AttributeError(meth)
        return lambda *a, **k: self.op(meth, *a, **k)

    def replay(self, eng):
        for meth, a, k, incs in self.q:
            instr = getattr(eng, meth)(*a, **k)
            for sem, v in incs:
                instr.then_inc(sem, v)


class Sem:
    def __init__(self, nc, stack, name):
        self.h = stack.enter_context(nc.semaphore(name))
        self.n = 0

    def inc(self, handle, k=1):
        handle.then_inc(self.h, k)
        self.n += k
        return self.n


class Ctx:
    def __init__(self):
        nc = bass.Bass("TRN2", target_bir_lowering=False)
        self.nc = nc
        self.stack = contextlib.ExitStack()
        self.sy = Rec("sync")
        self.g = Rec("gpsimd")
        self.v = Rec("vector")
        self.a = Rec("scalar")
        self.t = Rec("tensor")
        self.engs = [self.sy, self.g, self.v, self.a, self.t]
        self.dL = Sem(nc, self.stack, "dL")
        self.dG = Sem(nc, self.stack, "dG")
        self.dS = Sem(nc, self.stack, "dS")
        self.sV = Sem(nc, self.stack, "sV")
        self.sA = Sem(nc, self.stack, "sA")
        self.sT = Sem(nc, self.stack, "sT")
        self.sG = Sem(nc, self.stack, "sG")
        self.sC = Sem(nc, self.stack, "sC")
        self.ins = {}
        self.outs = {}
        self.sb = {}
        self.ps = {}
        self.dram = {}
        self.scoped = False
        self.static_names = set()
        self.gp = None
        self.gp_bump = 0

    def inp(self, name, shape, dtype=F32):
        if name not in self.ins:
            self.ins[name] = self.nc.dram_tensor(name, list(shape), dtype,
                                                 kind="ExternalInput").ap()
        return self.ins[name]

    def out(self, name, shape, dtype=F32):
        self.outs[name] = self.nc.dram_tensor(name, list(shape), dtype,
                                              kind="ExternalOutput").ap()
        return self.outs[name]

    def scratch(self, name, shape, dtype=F32):
        if name not in self.dram:
            self.dram[name] = self.nc.dram_tensor(name, list(shape), dtype,
                                                  kind="Internal").ap()
        return self.dram[name]

    GPSZ = 145664

    def sbuf(self, name, shape, dtype=F32, alias=None, off=0):
        if name in self.sb:
            return self.sb[name]
        if alias is not None:
            ar = self.sbuf(alias, [128, 16384], mybir.dt.uint8)
            nb = int(shape[1]) * mybir.dt.size(dtype)
            assert off + nb <= 16384, (name, off, nb)
            self.sb[name] = ar[:, off:off + nb].bitcast(dtype)[
                0:shape[0], 0:shape[1]]
        elif not self.scoped:
            self.sb[name] = self.stack.enter_context(
                self.nc.sbuf_tensor(name, list(shape), dtype))
            self.static_names.add(name)
        else:
            nb = int(shape[1]) * mybir.dt.size(dtype)
            a = (self.gp_bump + 31) // 32 * 32
            assert a + nb <= self.GPSZ, (name, a, nb, "GP arena overflow")
            self.gp_bump = a + nb
            gp = self.gp
            self.sb[name] = gp[:, a:a + nb].bitcast(dtype)[
                0:shape[0], 0:shape[1]]
        return self.sb[name]

    def scope_reset(self):
        for n in list(self.sb):
            if n not in self.static_names:
                del self.sb[n]
        self.gp_bump = 0

    def barrier(self):
        for eng in (self.sy, self.g, self.v, self.a, self.t):
            for s in (self.sV, self.sA, self.sT, self.sG, self.sC,
                      self.dL, self.dG, self.dS):
                if s.n:
                    eng.wait_ge(s.h, s.n)

    def free(self, name):
        if name in self.sb:
            h = self.sb.pop(name)
            try:
                h.free()
            except Exception:
                self.sb[name] = h

    def psum(self, name, shape, dtype=F32):
        if name not in self.ps:
            self.ps[name] = self.stack.enter_context(
                self.nc.psum_tensor(name, list(shape), dtype))
        return self.ps[name]

    def wait(self, eng, mark):
        if mark is not None:
            eng.wait_ge(mark[0].h, mark[1])

    def finish(self):
        nc = self.nc
        with self.stack:
            with nc.Block() as block:
                @block.sync
                def _(sync):
                    self.sy.replay(sync)

                @block.gpsimd
                def _(gpsimd):
                    self.g.replay(gpsimd)

                @block.vector
                def _(vector):
                    self.v.replay(vector)

                @block.scalar
                def _(scalar):
                    self.a.replay(scalar)

                @block.tensor
                def _(tensor):
                    self.t.replay(tensor)
        return nc



def _chunk128(vec):
    return np.ascontiguousarray(vec.reshape(-1, 128).T)


def stage_inputs(I):
    f = lambda a: np.ascontiguousarray(np.asarray(a, dtype=np.float32))
    x = f(I["L0_attn_a_x"]).reshape(16384, 1024)
    rel = f(I["L0_attn_a_rel_emb"])
    P0 = 4096
    Epad = np.empty((16, 16388), np.float32)
    for r in range(16):
        Epad[r, :P0] = rel[0, r]
        Epad[r, P0:P0 + 4096] = rel[:, r]
        Epad[r, P0 + 4096:] = rel[4095, r]
    EPLEN = 16000
    epads = []
    for cb in range(4):
        start = P0 + 4095 - 127 - 128 * cb - 7680
        assert start >= 0 and start + EPLEN <= Epad.shape[1]
        epads.append(np.ascontiguousarray(Epad[:, start:start + EPLEN]))

    tri = np.tril(np.ones((128, 128), np.float32), -1).T
    tri16 = np.zeros((16, 16), np.float32)
    for i in range(16):
        tri16[:i, i] = 1.0
    iot8 = np.broadcast_to(np.arange(8, dtype=np.float32), (128, 8)).copy()
    iden = np.eye(128, dtype=np.float32)
    m2T = np.tril(np.ones((128, 128), np.float32)).T * (1.0 / 512.0)
    m0T = np.tril(np.ones((128, 128), np.float32), -1) * (1.0 / 512.0)
    wq = f(I["L1_attn_a_wq"]).reshape(1024, 1024)
    wk = f(I["L1_attn_a_wk"]).reshape(1024, 1024)
    wv = f(I["L1_attn_a_wv"]).reshape(1024, 1024)
    wo = f(I["L1_attn_a_wo"]).reshape(1024, 1024)

    staged = []
    for c in range(NCORE):
        cb = c % 4
        d = {
            "xT": np.ascontiguousarray(x[2048 * c:2048 * (c + 1), :].T),
            "iden": iden, "tri": tri, "tri16": tri16, "iot8": iot8,
            "selrank": (np.arange(8) < c).astype(np.float32)[None, :],
            "selrank16": np.ascontiguousarray(np.broadcast_to(
                (np.arange(8) < c).astype(np.float32)[:, None], (8, 16))),
            "epad": epads[cb], "m2T": m2T, "m0T": m0T,
            "Winq": f(I["L0_attn_a_W_in"][:, 0:1024]),
            "Wink": f(I["L0_attn_a_W_in"][:, 1024:2048]),
            "Winv": f(I["L0_attn_a_W_in"][:, 2048:3072]),
            "binq_c": _chunk128(f(I["L0_attn_a_b_in"][0:1024])),
            "bink_c": _chunk128(f(I["L0_attn_a_b_in"][1024:2048])),
            "binv_r": f(I["L0_attn_a_b_in"][2048:3072]).reshape(1, 1024),
            "Wout": f(I["L0_attn_a_W_out"]),
            "bout_c": _chunk128(f(I["L0_attn_a_b_out"])),
            "w0": f(I["L0_ffn_weight"]),
            "b0c": _chunk128(f(I["L0_ffn_bias"])),
            "Wqkv": f(I["L0_attn_b_W_qkv"]),
            "bqkv_c": np.ascontiguousarray(
                f(I["L0_attn_b_b_qkv"]).reshape(-1, 128).T),
            "bqkv_vr": f(I["L0_attn_b_b_qkv"][2048:3072]).reshape(1, 1024),
            "haloidx": (np.arange(128)[:, None] + 128 * np.arange(8)[None, :]
                        + 1024 * max((c % 4) - 1, 0)).astype(np.int32),
            "hmask": np.full((128, 1), 1.0 if (c % 4) != 0 else 0.0, np.float32),
            "lnw_c": _chunk128(f(I["L0_attn_b_ln_w"])),
            "lnb_c": _chunk128(f(I["L0_attn_b_ln_b"])),
            "g1c": _chunk128(f(I["L1_attn_a_gamma1"])),
            "be1c": _chunk128(f(I["L1_attn_a_beta1"])),
            "wqf": wq, "wkf": wk, "wvf": wv, "wof": wo,
            "w1i": np.ascontiguousarray(
                f(I["L1_attn_a_w1"]).reshape(8, 128, 32, 128)
                .transpose(2, 1, 0, 3)),
            "w2i": np.ascontiguousarray(
                f(I["L1_attn_a_w2"]).reshape(32, 128, 8, 128)
                .transpose(2, 1, 0, 3)),
            "b1c": np.ascontiguousarray(
                f(I["L1_attn_a_b1"]).reshape(-1, 128).T),
            "b2c": _chunk128(f(I["L1_attn_a_b2"])),
            "Wg": f(I["L1_ffn_Wg"]),
            "We": f(I["L1_ffn_We"]),
            "bec": _chunk128(f(I["L1_ffn_be"])),
            "WqT7": f(I["L1_attn_b_Wq"].T),
            "WkT7": f(I["L1_attn_b_Wk"].T),
            "WvT7": f(I["L1_attn_b_Wv"].T),
            "bq7c": _chunk128(f(I["L1_attn_b_bq"])),
            "bk7c": _chunk128(f(I["L1_attn_b_bk"])),
            "bv7c": _chunk128(f(I["L1_attn_b_bv"])),
            "bv7r": f(I["L1_attn_b_bv"]).reshape(1, 1024),
        }
        staged.append(d)
    return staged



def prologue(C):
    xdram = C.inp("xT", [D, T])
    xT = [C.sbuf(f"xT{i}", [128, T]) for i in range(ND)]
    for i in range(ND):
        C.dL.inc(C.sy.dma_start(out=xT[i][:, :],
                                in_=xdram[128 * i:128 * (i + 1), :]), 16)
    idn = C.inp("iden", [128, 128])
    idf = C.sbuf("idf", [128, 128], F32)
    idb = C.sbuf("idb", [128, 128], BF16)
    C.dL.inc(C.sy.dma_start(out=idf[:, :], in_=idn), 16)
    C.dG.inc(C.g.dma_start(out=idb[:, :], in_=idn), 16)
    C.v.memset(C.sbuf("onesf", [1, 128], F32)[:, :], 1.0)
    C.v.memset(C.sbuf("onescf", [128, 1], F32)[:, :], 1.0)
    C.v.memset(C.sbuf("onescb", [128, 1], BF16)[:, :], 1.0)
    mk = C.sV.inc(C.v.memset(C.sbuf("onesb", [1, 128], BF16)[:, :], 1.0))
    for e in (C.v, C.a, C.t, C.g):
        e.wait_ge(C.dL.h, C.dL.n)
        e.wait_ge(C.dG.h, C.dG.n)
        e.wait_ge(C.sV.h, mk)
    return xT


def epilogue(C, xT, mark):
    ydram = C.out("yT", [D, T])
    C.wait(C.sy, mark)
    for i in range(ND):
        C.dS.inc(C.sy.dma_start(out=ydram[128 * i:128 * (i + 1), :],
                                in_=xT[i][:, :]), 16)
    C.sy.wait_ge(C.dS.h, C.dS.n)


def get_pb(C):
    big = C.psum("pbig", [128, 2048], F32)
    out = [big[:, 512 * i:512 * (i + 1)] for i in range(4)]
    out += [C.psum(f"pb{i}", [128, 512], F32) for i in range(4, 8)]
    out.append(big[:, :])
    return out


def emit_ln(C, xT, out_bf, mark, gamma=None, beta=None, eps=1e-5, apply=True, f32src=True):
    pb = get_pb(C)
    onesf = C.sb["onesf"]
    rows = C.sbuf("ln_rows", [1, 2 * T], F32, alias="scrA")
    sqc = [C.sbuf(f"ln_sqc{j}", [128, 512], BF16) for j in range(2)]
    tmpc = C.sbuf("ln_tmpc", [128, 512], F32)

    C.wait(C.v, mark)
    C.wait(C.t, mark)
    for qc in range(4):
        s_ = slice(512 * qc, 512 * (qc + 1))
        if qc > 0:
            C.t.wait_ge(C.sV.h, C.sV.n)
        onescol = C.sb["onescf"] if f32src else C.sb["onescb"]
        for i in range(ND):
            C.t.matmul(pb[4][0:1, :], onescol[:, :], xT[i][:, s_],
                       start=(i == 0), stop=(i == ND - 1))
        for i in range(ND):
            j = i % 2
            vm = C.sV.inc(C.v.tensor_mul(sqc[j][:, :], xT[i][:, s_], xT[i][:, s_]))
            C.t.wait_ge(C.sV.h, vm)
            tm = C.sT.inc(C.t.matmul(pb[5][0:1, :], C.sb["onescb"][:, :], sqc[j][:, :],
                                     start=(i == 0), stop=(i == ND - 1)))
            C.v.wait_ge(C.sT.h, tm)
        C.v.wait_ge(C.sT.h, C.sT.n)
        C.v.tensor_copy(rows[0:1, s_], pb[4][0:1, :])
        C.sV.inc(C.v.tensor_copy(rows[0:1, T + 512 * qc:T + 512 * (qc + 1)],
                                 pb[5][0:1, :]))
    C.v.wait_ge(C.sV.h, C.sV.n)
    vm = C.sV.inc(C.v.tensor_scalar(rows[0:1, 0:T], rows[0:1, 0:T],
                                    1.0 / 1024.0, None, ALU.mult))
    for qc in range(4):
        s_ = slice(512 * qc, 512 * (qc + 1))
        sT_ = slice(T + 512 * qc, T + 512 * (qc + 1))
        C.v.wait_ge(C.sV.h, C.sV.n)
        vm = C.sV.inc(C.v.tensor_mul(tmpc[0:1, :], rows[0:1, s_], rows[0:1, s_]))
        C.v.wait_ge(C.sV.h, vm)
        vm = C.sV.inc(C.v.tensor_scalar(rows[0:1, sT_], rows[0:1, sT_],
                                        1.0 / 1024.0, eps, ALU.mult, ALU.add))
        C.v.wait_ge(C.sV.h, vm)
        vm = C.sV.inc(C.v.tensor_sub(rows[0:1, sT_], rows[0:1, sT_], tmpc[0:1, :]))
    C.a.wait_ge(C.sV.h, C.sV.n)
    am = C.sA.inc(C.a.activation(rows[0:1, T:2 * T], rows[0:1, T:2 * T], AF.Ln))
    C.a.wait_ge(C.sA.h, am)
    am = C.sA.inc(C.a.activation(rows[0:1, T:2 * T], rows[0:1, T:2 * T],
                                 AF.Exp, scale=-0.5))
    C.v.wait_ge(C.sA.h, am)
    vm = C.sV.inc(C.v.scalar_tensor_tensor(rows[0:1, 0:T], rows[0:1, 0:T],
                                           -1.0, rows[0:1, T:2 * T],
                                           ALU.mult, ALU.mult))
    ab = rows
    if not apply:
        return (C.sV, vm)
    C.t.wait_ge(C.sV.h, vm)
    for qc in range(4):
        s_ = slice(512 * qc, 512 * (qc + 1))
        if qc > 0:
            C.t.wait_ge(C.sV.h, C.sV.n)
        C.t.matmul(pb[6][:, :], onesf[:, :],
                   ab[0:1, T + 512 * qc:T + 512 * (qc + 1)],
                   start=True, stop=True)
        tm = C.sT.inc(C.t.matmul(pb[7][:, :], onesf[:, :], ab[0:1, s_],
                                 start=True, stop=True))
        C.v.wait_ge(C.sT.h, tm)
        for i in range(ND):
            vm = C.sV.inc(C.v.tensor_tensor(tmpc[:, :], xT[i][:, s_],
                                            pb[6][:, :], ALU.mult))
            C.v.wait_ge(C.sV.h, vm)
            C.sV.inc(C.v.tensor_tensor(out_bf[i][:, s_], tmpc[:, :],
                                       pb[7][:, :], ALU.add))
    lastmark = (C.sV, C.sV.n)
    if gamma is not None:
        gm = C.sb[gamma]
        bt = C.sb[beta]
        C.v.wait_ge(C.sV.h, C.sV.n)
        for i in range(ND):
            h = C.v.tensor_scalar(out_bf[i][:, :], out_bf[i][:, :],
                                  gm[:, i:i + 1], bt[:, i:i + 1],
                                  ALU.mult, ALU.add)
        lastmark = (C.sV, C.sV.inc(h))
    return lastmark


def emit_transpose_f2t(C, srcT, dst_tok, tile_idx, mark, dtype_note="f32"):
    pb = get_pb(C)
    idf = C.sb["idf"]
    s_ = slice(128 * tile_idx, 128 * (tile_idx + 1))
    C.wait(C.t, mark)
    for i in range(ND):
        if i > 0:
            C.t.wait_ge(C.sV.h, C.sV.n)
        tm = C.sT.inc(C.t.transpose(pb[3][:, 0:128], srcT[i][:, s_], idf[:, :]))
        C.v.wait_ge(C.sT.h, tm)
        C.sV.inc(C.v.tensor_copy(dst_tok[:, 128 * i:128 * (i + 1)],
                                 pb[3][:, 0:128]))
    return (C.sV, C.sV.n)


def emit_quant_tile(C, y, q_out, mark, scr):
    v = C.v
    sm = scr["sm"]
    sr = scr["sr"]
    t0 = scr["t0"]
    t1 = scr["t1"]
    t2 = scr["t2"]
    ti = scr["ti"]
    C.wait(v, mark)

    def step(h):
        return C.sV.inc(h)

    def chain(h):
        n = C.sV.inc(h)
        v.wait_ge(C.sV.h, n)
        return n

    g32 = y[:, :].rearrange("p (g k) -> p g k", k=32)
    chain(v.tensor_reduce(sm[:, :], g32, AX.X, ALU.max, apply_absolute_value=True))
    chain(v.tensor_scalar(sm[:, :], sm[:, :], float(np.float32(1.0 / 7.5)),
                          1e-30, ALU.mult, ALU.max))
    chain(v.reciprocal(sr[:, :], sm[:, :]))
    _s = sr[:, :]
    rsb = bass.AP(_s.tensor, _s.offset, [list(_s.ap)[0], [1, 32], [0, 32]])
    chain(v.tensor_tensor(t0[:, :].rearrange("p (g k) -> p g k", k=32),
                          g32, rsb, ALU.mult))
    stp = scr["stp"]
    chain(v.tensor_mul(t1[:, :], t0[:, :], t0[:, :]))
    chain(v.tensor_scalar(t2[:, :], t1[:, :], 4.0, None, ALU.is_gt))
    chain(v.tensor_scalar(t1[:, :], t1[:, :], 16.0, None, ALU.is_gt))
    chain(v.tensor_scalar(stp[:, :], t2[:, :], 0.125, 0.125, ALU.mult, ALU.add))
    chain(v.scalar_tensor_tensor(stp[:, :], t1[:, :], 0.25, stp[:, :],
                                 ALU.mult, ALU.add))
    chain(v.tensor_scalar(t2[:, :], t2[:, :], -4.0, 8.0, ALU.mult, ALU.add))
    chain(v.scalar_tensor_tensor(t2[:, :], t1[:, :], -2.0, t2[:, :],
                                 ALU.mult, ALU.add))
    chain(v.tensor_tensor(t2[:, :], t0[:, :], t2[:, :], ALU.mult))
    chain(v.tensor_scalar(t2[:, :], t2[:, :], -1.0, 0.5, ALU.mult, ALU.add))
    chain(v.tensor_copy(ti[:, :], t2[:, :]))
    chain(v.tensor_copy(t0[:, :], ti[:, :]))
    chain(v.tensor_tensor(t1[:, :], t0[:, :], t2[:, :], ALU.is_gt))
    chain(v.tensor_tensor(t0[:, :], t0[:, :], t1[:, :], ALU.subtract))
    chain(v.tensor_scalar(t0[:, :], t0[:, :], -1.0, None, ALU.mult))
    chain(v.tensor_tensor(t0[:, :], t0[:, :], stp[:, :], ALU.mult))
    _m = sm[:, :]
    smb = bass.AP(_m.tensor, _m.offset, [list(_m.ap)[0], [1, 32], [0, 32]])
    chain(v.tensor_tensor(q_out[:, :].rearrange("p (g k) -> p g k", k=32),
                          t0[:, :].rearrange("p (g k) -> p g k", k=32),
                          smb, ALU.mult))
    return (C.sV, C.sV.n)


def emit_s2(C, xT, mark, tag):
    pb = get_pb(C)
    idb = C.sb["idb"]
    lnout = None
    mark = emit_ln(C, xT, lnout, mark, apply=False)
    ab = C.sb["ln_rows"]
    abt = C.sbuf("s2_abt", [128, 2 * NT], F32)
    abd = C.scratch("s2_abd", [1, 2 * T])
    C.wait(C.sy, mark)
    C.dL.inc(C.sy.dma_start(out=abd, in_=ab[:, :]), 16)
    C.sy.wait_ge(C.dL.h, C.dL.n)
    C.dL.inc(C.sy.dma_start(
        out=abt[:, 0:NT],
        in_=bass.AP(abd.tensor, T, [[1, 128], [128, NT]]),
        allow_slow_non_contiguous=True), 16)
    C.dL.inc(C.sy.dma_start(
        out=abt[:, NT:2 * NT],
        in_=bass.AP(abd.tensor, 0, [[1, 128], [128, NT]]),
        allow_slow_non_contiguous=True), 16)
    abmark = (C.dL, C.dL.n)

    wdT = [C.sbuf(f"s2_wdT{i}", [128, D], BF16) for i in range(ND)]
    d_wdT = C.scratch("s2_dwdT", [D, D], BF16)
    w0 = C.inp("w0", [D, D])
    wf = C.sbuf("s2_wf", [128, D], F32)
    wq_ = C.sbuf("s2_wq", [128, D], BF16)
    scr = {
        "sm": C.sbuf("q_sm", [128, 32], F32),
        "sr": C.sbuf("q_sr", [128, 32], F32),
        "t0": C.sbuf("q_t0", [128, D], F32),
        "t1": C.sbuf("q_t1", [128, D], F32),
        "t2": C.sbuf("q_t2", [128, D], F32),
        "ti": C.sbuf("q_ti", [128, D], I32),
        "stp": C.sbuf("q_stp", [128, D], F32),
    }
    wm = None
    tpm = {}
    if tag == "s4":
        for ki in range(ND):
            C.dL.inc(C.sy.dma_start(out=wdT[ki][:, :],
                                    in_=d_wdT[128 * ki:128 * (ki + 1), :]), 16)
        wlm = C.dL.n
        for e in (C.v, C.t):
            e.wait_ge(C.dL.h, wlm)
        wm = (C.dL, wlm)
    else:
      for oi in range(ND):
        C.wait(C.sy, wm)
        ld = C.dL.inc(C.sy.dma_start(out=wf[:, :],
                                     in_=w0[128 * oi:128 * (oi + 1), :]), 16)
        wm = emit_quant_tile(C, wf, wq_, (C.dL, ld), scr)
        C.wait(C.t, wm)
        for ki in range(ND):
            bx = 2 + (ki % 2)
            C.wait(C.t, tpm.get(bx))
            tm = C.sT.inc(C.t.transpose(
                pb[bx][:, 0:256].bitcast(BF16)[:, 0:128],
                wq_[:, 128 * ki:128 * (ki + 1)], idb[:, :]))
            C.v.wait_ge(C.sT.h, tm)
            vm = C.sV.inc(C.v.tensor_copy(wdT[ki][:, 128 * oi:128 * (oi + 1)],
                                          pb[bx][:, 0:256].bitcast(BF16)[:, 0:128]))
            tpm[bx] = (C.sV, vm)
        wm = (C.sV, C.sV.n)
    if tag != "s4":
        C.wait(C.sy, wm)
        for ki in range(ND):
            C.dL.inc(C.sy.dma_start(out=d_wdT[128 * ki:128 * (ki + 1), :],
                                    in_=wdT[ki][:, :]), 16)

    xdT = [C.sbuf(f"s2_xdT{i}", [128, T], BF16) for i in range(ND)]
    xtok = C.sbuf("s2_xtok", [128, D], F32)
    xqt = C.sbuf("s2_xq", [128, D], BF16)
    qm = wm
    for tt in range(NT):
        qm = emit_transpose_f2t(C, xT, xtok, tt, qm)
        C.wait(C.v, qm)
        C.wait(C.v, abmark)
        vm = C.sV.inc(C.v.tensor_scalar(
            xtok[:, :], xtok[:, :], abt[:, tt:tt + 1],
            abt[:, NT + tt:NT + tt + 1], ALU.mult, ALU.add))
        qm = emit_quant_tile(C, xtok, xqt, (C.sV, vm), scr)
        C.wait(C.t, qm)
        for ki in range(ND):
            bx = 2 + (ki % 2)
            C.wait(C.t, tpm.get(bx))
            tm = C.sT.inc(C.t.transpose(
                pb[bx][:, 0:256].bitcast(BF16)[:, 0:128],
                xqt[:, 128 * ki:128 * (ki + 1)], idb[:, :]))
            C.v.wait_ge(C.sT.h, tm)
            vm = C.sV.inc(C.v.tensor_copy(xdT[ki][:, 128 * tt:128 * (tt + 1)],
                                          pb[bx][:, 0:256].bitcast(BF16)[:, 0:128]))
            tpm[bx] = (C.sV, vm)
        qm = (C.sV, C.sV.n)

    b0c = C.sbuf("s2_b0c", [128, ND], F32)
    C.dL.inc(C.sy.dma_start(out=b0c[:, :], in_=C.inp("b0c", [128, ND])), 16)
    bm = (C.dL, C.dL.n)
    C.wait(C.t, qm)
    C.wait(C.v, bm)
    mmk = {}
    for oi in range(ND):
        for qc in range(4):
            bx = qc % 2
            bank = pb[bx]
            s_ = slice(512 * qc, 512 * (qc + 1))
            C.wait(C.t, mmk.get(bx))
            for ki in range(ND):
                h = C.t.matmul(bank[:, :], wdT[ki][:, 128 * oi:128 * (oi + 1)],
                               xdT[ki][:, s_], start=(ki == 0), stop=(ki == ND - 1))
            tm = C.sT.inc(h)
            C.v.wait_ge(C.sT.h, tm)
            vm = C.sV.inc(C.v.scalar_tensor_tensor(
                xT[oi][:, s_], bank[:, :], b0c[:, oi:oi + 1], xT[oi][:, s_],
                ALU.add, ALU.add))
            mmk[bx] = (C.sV, vm)
    return (C.sV, C.sV.n)



EMITTERS = {}


def build_program(subset):
    C = Ctx()
    xT = prologue(C)
    get_pb(C)
    C.gp = C.stack.enter_context(
        C.nc.sbuf_tensor("GP", [128, Ctx.GPSZ], mybir.dt.uint8))
    C.static_names.add("GP")
    C.scoped = True
    mark = None
    for k in subset:
        C.scope_reset()
        C.barrier()
        mark = EMITTERS[k](C, xT, mark)
    C.barrier()
    epilogue(C, xT, mark)
    return C


_PROG_CACHE = {}


def _get_prog(subset):
    key = tuple(subset)
    if key not in _PROG_CACHE:
        C = build_program(subset)
        nc = C.finish()
        _PROG_CACHE[key] = (nc, sorted(C.ins.keys()))
    return _PROG_CACHE[key]


def run_subset(subset, staged):
    nc, in_names = _get_prog(subset)
    in_maps = [{k: staged[c][k] for k in in_names} for c in range(NCORE)]
    res = run_bass_kernel_spmd(nc, in_maps, core_ids=list(range(NCORE)))
    for c in range(NCORE):
        staged[c]["xT"] = np.ascontiguousarray(res.results[c]["yT"])
    return staged


LAUNCH_PLAN = [[1, 2, 3, 4, 5, 6, 7, 8]]


def kernel(**inputs):
    staged = stage_inputs(inputs)
    for subset in LAUNCH_PLAN:
        staged = run_subset(subset, staged)
    out = np.empty((16384, 1024), np.float32)
    for c in range(NCORE):
        out[2048 * c:2048 * (c + 1), :] = staged[c]["xT"].T
    return out.reshape(2, 8192, 1024)


def _register():
    EMITTERS[2] = lambda C, xT, mark: emit_s2(C, xT, mark, "s2")
    EMITTERS[7] = lambda C, xT, mark: emit_s7(C, xT, mark)
    EMITTERS[6] = lambda C, xT, mark: emit_s6(C, xT, mark, "s6")
    EMITTERS[3] = lambda C, xT, mark: emit_s3(C, xT, mark)
    EMITTERS[8] = lambda C, xT, mark: emit_s6(C, xT, mark, "s8")
    EMITTERS[4] = lambda C, xT, mark: emit_s2(C, xT, mark, "s4")


_register()


def emit_s7(C, xT, mark, tag="s7"):
    pb = get_pb(C)
    onesb = C.sb["onesb"]
    xn = [C.sbuf(f"xn{i}", [128, T], BF16) for i in range(ND)]
    mark = emit_ln(C, xT, xn, mark)

    w7 = {}
    for nm in ("WqT7", "WkT7", "WvT7"):
        dr = C.inp(nm, [D, D])
        w7[nm] = [C.sbuf(f"{nm}_{i}", [128, D], BF16) for i in range(ND)]
        for i in range(ND):
            C.dG.inc(C.g.dma_start(out=w7[nm][i][:, :],
                                   in_=dr[128 * i:128 * (i + 1), :]), 16)
    bq = C.sbuf("s7_bq", [128, ND], F32)
    bk = C.sbuf("s7_bk", [128, ND], F32)
    bvr = C.sbuf("s7_bvr", [1, D], BF16)
    C.dL.inc(C.sy.dma_start(out=bq[:, :], in_=C.inp("bq7c", [128, ND])), 16)
    C.dL.inc(C.sy.dma_start(out=bk[:, :], in_=C.inp("bk7c", [128, ND])), 16)
    C.dG.inc(C.g.dma_start(out=bvr[:, :], in_=C.inp("bv7r", [1, D])), 16)
    for e in (C.v, C.a, C.t):
        e.wait_ge(C.dL.h, C.dL.n)
        e.wait_ge(C.dG.h, C.dG.n)

    qT = [C.sbuf(f"s7_qT{i}", [128, 512], BF16) for i in range(ND)]
    kT = [C.sbuf(f"s7_kT{i}", [128, 512], BF16) for i in range(ND)]
    vtk = [C.sbuf(f"s7_v{i}", [128, D], BF16) for i in range(4)]
    attn = C.sbuf("s7_attn", [128, 512], BF16)
    attnT = C.sbuf("s7_attnT", [128, 512], BF16)
    mx = C.sbuf("s7_mx", [128, 1], F32)
    nm_ = C.sbuf("s7_nm", [128, 1], F32)
    asum = C.sbuf("s7_as", [128, 1], F32)
    rs = C.sbuf("s7_rs", [128, 1], F32)
    idb = C.sb["idb"]

    C.wait(C.t, mark)
    s7m = {}
    for cn in range(4):
        s7 = slice(512 * cn, 512 * (cn + 1))
        for ei in range(ND):
            for (wnm, dstl, bc) in (("WqT7", qT, bq), ("WkT7", kT, bk)):
                bx = 0 if wnm == "WqT7" else 1
                bank = pb[bx]
                C.wait(C.t, s7m.get(bx))
                for di in range(ND):
                    h = C.t.matmul(bank[:, :],
                                   w7[wnm][di][:, 128 * ei:128 * (ei + 1)],
                                   xn[di][:, s7], start=(di == 0), stop=(di == ND - 1))
                tm = C.sT.inc(h)
                C.v.wait_ge(C.sT.h, tm)
                vm = C.sV.inc(C.v.tensor_scalar(dstl[ei][:, :], bank[:, :],
                                                bc[:, ei:ei + 1], None, ALU.add))
                s7m[bx] = (C.sV, vm)
        for ts_ in range(4):
            tsl = slice(512 * cn + 128 * ts_, 512 * cn + 128 * (ts_ + 1))
            for eh in range(2):
                esl = slice(512 * eh, 512 * (eh + 1))
                bx = (2, 7)[(2 * ts_ + eh) % 2]
                bank = pb[bx]
                C.wait(C.t, s7m.get(bx))
                for di in range(ND):
                    C.t.matmul(bank[:, :], xn[di][:, tsl],
                               w7["WvT7"][di][:, esl], start=(di == 0), stop=False)
                h = C.t.matmul(bank[:, :], onesb[:, :], bvr[:, esl],
                               start=False, stop=True)
                tm = C.sT.inc(h)
                C.v.wait_ge(C.sT.h, tm)
                vm = C.sV.inc(C.v.tensor_copy(vtk[ts_][:, esl], bank[:, :]))
                s7m[bx] = (C.sV, vm)
        vmark = C.sV.n
        for qs in range(4):
            qsl = slice(128 * qs, 128 * (qs + 1))
            C.wait(C.t, s7m.get(0))
            C.wait(C.t, s7m.get(1))
            C.wait(C.t, s7m.get(3))
            for ei in range(ND):
                h = C.t.matmul(pb[3][:, :], qT[ei][:, qsl], kT[ei][:, :],
                               start=(ei == 0), stop=(ei == ND - 1))
            tm = C.sT.inc(h)
            C.v.wait_ge(C.sT.h, tm)
            vm = C.sV.inc(C.v.reduce_max(mx[:, :], pb[3][:, :], AX.X))
            C.v.wait_ge(C.sV.h, vm)
            vm = C.sV.inc(C.v.tensor_scalar(nm_[:, :], mx[:, :], -1.0, None,
                                            ALU.mult))
            C.a.wait_ge(C.sV.h, vm)
            am = C.sA.inc(C.a.activation(attn[:, :], pb[3][:, :], AF.Exp,
                                         bias=nm_[:, :], accum_out=asum[:, :]))
            s7m[3] = (C.sA, am)
            C.v.wait_ge(C.sA.h, am)
            vm = C.sV.inc(C.v.reciprocal(rs[:, :], asum[:, :]))
            C.v.wait_ge(C.sV.h, vm)
            vm = C.sV.inc(C.v.tensor_scalar(attn[:, :], attn[:, :], rs[:, :],
                                            None, ALU.mult))
            C.t.wait_ge(C.sV.h, vm)
            for kk in range(4):
                if kk > 0:
                    C.t.wait_ge(C.sV.h, C.sV.n)
                tm = C.sT.inc(C.t.transpose(
                    pb[4][:, 0:256].bitcast(BF16)[:, 0:128],
                    attn[:, 128 * kk:128 * (kk + 1)], idb[:, :]))
                C.v.wait_ge(C.sT.h, tm)
                C.sV.inc(C.v.tensor_copy(attnT[:, 128 * kk:128 * (kk + 1)],
                                         pb[4][:, 0:256].bitcast(BF16)[:, 0:128]))
            C.t.wait_ge(C.sV.h, C.sV.n)
            for ei in range(ND):
                bank = pb[5] if ei % 2 == 0 else pb[6]
                if ei >= 2:
                    C.t.wait_ge(C.sV.h, C.sV.n)
                for kk in range(4):
                    h = C.t.matmul(bank[:, 0:128],
                                   vtk[kk][:, 128 * ei:128 * (ei + 1)],
                                   attnT[:, 128 * kk:128 * (kk + 1)],
                                   start=(kk == 0), stop=(kk == 3))
                tm = C.sT.inc(h)
                C.v.wait_ge(C.sT.h, tm)
                C.sV.inc(C.v.tensor_tensor(
                    xT[ei][:, 512 * cn + 128 * qs:512 * cn + 128 * (qs + 1)],
                    xT[ei][:, 512 * cn + 128 * qs:512 * cn + 128 * (qs + 1)],
                    bank[:, 0:128], ALU.add))
    return (C.sV, C.sV.n)


def emit_s6(C, xT, mark, tag="s6"):
    pb = get_pb(C)
    onesf = C.sb["onesf"]
    idf = C.sb["idf"]
    xn = [C.sbuf(f"xn{i}", [128, T], BF16) for i in range(ND)]
    mark = emit_ln(C, xT, xn, mark)

    Wg = C.inp("Wg", [D, 8])
    wg = [C.sbuf(f"s6_wg{i}", [128, 8], BF16) for i in range(ND)]
    for i in range(ND):
        C.dG.inc(C.g.dma_start(out=wg[i][:, :], in_=Wg[128 * i:128 * (i + 1), :]), 16)
    We = C.inp("We", [D, D])
    we = [C.sbuf(f"s6_we{i}", [128, D], BF16) for i in range(ND)]
    for i in range(ND):
        C.dG.inc(C.g.dma_start(out=we[i][:, :], in_=We[128 * i:128 * (i + 1), :]), 16)
    trib = C.sbuf("s6_trib", [128, 128], BF16)
    C.dG.inc(C.g.dma_start(out=trib[:, :], in_=C.inp("tri", [128, 128])), 16)
    tri16 = C.sbuf("s6_tri16", [16, 16], F32)
    C.dL.inc(C.sy.dma_start(out=tri16[:, :], in_=C.inp("tri16", [16, 16])), 16)
    iot8 = C.sbuf("s6_iot8", [128, 8], F32)
    C.dL.inc(C.sy.dma_start(out=iot8[:, :], in_=C.inp("iot8", [128, 8])), 16)
    sel16 = C.sbuf("s6_sel16", [8, 16], F32)
    C.dL.inc(C.sy.dma_start(out=sel16[:, :], in_=C.inp("selrank16", [8, 16])), 16)
    bec = C.sbuf("s6_bec", [128, ND], F32)
    C.dL.inc(C.sy.dma_start(out=bec[:, :], in_=C.inp("bec", [128, ND])), 16)
    for e in (C.v, C.a, C.t, C.g):
        e.wait_ge(C.dL.h, C.dL.n)
        e.wait_ge(C.dG.h, C.dG.n)

    lgT = C.sbuf("s6_lgT", [8, T], F32)
    lgt = C.sbuf("s6_lgt", [128, 8], F32)
    Mb = [C.sbuf(f"s6_M{t}", [128, 8], BF16) for t in range(NT)]
    gate = C.sbuf("s6_gate", [128, NT], F32)
    estar = C.sbuf("s6_est", [128, 1], F32)
    mx = C.sbuf("s6_mx", [128, 1], F32)
    tmp8 = C.sbuf("s6_t8", [128, 8], F32)
    cntrow = C.sbuf("s6_cntr", [1, 128], F32)
    cnt16 = C.sbuf("s6_c16", [16, 8], F32)
    gath = C.sbuf("s6_gath", [8, 8], F32)
    offsetG = C.sbuf("s6_offG", [1, 8], F32)
    prefall = C.sbuf("s6_pref", [16, 8], F32)
    coefc = C.sbuf("s6_coefc", [128, NT], F32)
    coefrow = C.sbuf("s6_crow", [1, T], F32)
    tmpe = C.sbuf("s6_tmpe", [128, 512], F32)
    d_cc_in = C.scratch(f"{tag}_ccin", [1, 8])
    d_cc_out = C.scratch(f"{tag}_ccout", [8, 8])
    d_cnt = C.scratch(f"{tag}_cnt", [1, 128])
    d_coef = C.scratch(f"{tag}_coef", [128, NT])

    C.wait(C.t, mark)
    for qc in range(4):
        s_ = slice(512 * qc, 512 * (qc + 1))
        if qc:
            C.t.wait_ge(C.sV.h, C.sV.n)
        for di in range(ND):
            h = C.t.matmul(pb[0][0:8, :], wg[di][:, :], xn[di][:, s_],
                           start=(di == 0), stop=(di == ND - 1))
        tm = C.sT.inc(h)
        C.v.wait_ge(C.sT.h, tm)
        C.sV.inc(C.v.tensor_copy(lgT[:, s_], pb[0][0:8, :]))
    for t in range(NT):
        tsl = slice(128 * t, 128 * (t + 1))
        C.t.wait_ge(C.sV.h, C.sV.n)
        tm = C.sT.inc(C.t.transpose(pb[1][0:128, 0:8], lgT[0:8, tsl],
                                    idf[0:8, 0:8]))
        C.v.wait_ge(C.sT.h, tm)
        vm = C.sV.inc(C.v.tensor_copy(lgt[:, :], pb[1][0:128, 0:8]))
        C.v.wait_ge(C.sV.h, vm)
        vm = C.sV.inc(C.v.reduce_max(mx[:, :], lgt[:, :], AX.X))
        C.v.wait_ge(C.sV.h, vm)
        vm = C.sV.inc(C.v.tensor_scalar(tmp8[:, :], lgt[:, :], mx[:, :], None,
                                        ALU.subtract))
        C.a.wait_ge(C.sV.h, vm)
        am = C.sA.inc(C.a.activation(tmp8[:, :], tmp8[:, :], AF.Exp,
                                     accum_out=estar[:, :]))
        C.v.wait_ge(C.sA.h, am)
        vm = C.sV.inc(C.v.reciprocal(gate[:, t:t + 1], estar[:, :]))
        C.v.wait_ge(C.sV.h, vm)
        vm = C.sV.inc(C.v.tensor_scalar(tmp8[:, :], lgt[:, :], mx[:, :], None,
                                        ALU.is_equal))
        C.v.wait_ge(C.sV.h, vm)
        C.sV.inc(C.v.tensor_copy(Mb[t][:, :], tmp8[:, :]))
        vm = C.sV.inc(C.v.tensor_scalar(tmp8[:, :], tmp8[:, :], -999.0, 999.0,
                                        ALU.mult, ALU.add))
        C.v.wait_ge(C.sV.h, vm)
        vm = C.sV.inc(C.v.tensor_add(tmp8[:, :], tmp8[:, :], iot8[:, :]))
        C.v.wait_ge(C.sV.h, vm)
        vm = C.sV.inc(C.v.tensor_reduce(estar[:, :], tmp8[:, :], AX.X, ALU.min))
        C.v.wait_ge(C.sV.h, vm)
        vm = C.sV.inc(C.v.tensor_scalar(Mb[t][:, :], iot8[:, :], estar[:, :],
                                        None, ALU.is_equal))
        C.t.wait_ge(C.sV.h, vm)
        h = C.t.matmul(pb[2][0:1, 8 * t:8 * (t + 1)], C.sb["onescb"][:, :],
                       Mb[t][:, :], start=(t == 0), stop=(t == NT - 1))
    tm = C.sT.inc(h)
    C.v.wait_ge(C.sT.h, tm)
    vm = C.sV.inc(C.v.tensor_copy(cntrow[:, :], pb[2][0:1, 0:128]))
    C.v.wait_ge(C.sV.h, vm)
    _c = cntrow[0:1, :]
    c3 = bass.AP(_c.tensor, _c.offset, [list(_c.ap)[0], [1, 8], [8, 16]])
    vm = C.sV.inc(C.v.tensor_reduce(offsetG[:, :], c3, AX.X, ALU.add))
    C.wait(C.sy, (C.sV, vm))
    C.dL.inc(C.sy.dma_start(out=d_cc_in, in_=offsetG[:, :]), 16)
    C.g.wait_ge(C.dL.h, C.dL.n)
    cm = C.sC.inc(C.g.collective_compute(
        "AllGather", ALU.bypass, replica_groups=[list(range(8))],
        ins=[d_cc_in.opt()], outs=[d_cc_out.opt()]))
    C.sy.wait_ge(C.sC.h, cm)
    C.dL.inc(C.sy.dma_start(out=gath[:, :], in_=d_cc_out), 16)
    C.dL.inc(C.sy.dma_start(out=d_cnt, in_=cntrow[:, :]), 16)
    C.sy.wait_ge(C.dL.h, C.dL.n)
    C.dL.inc(C.sy.dma_start(
        out=cnt16[:, :], in_=bass.AP(d_cnt.tensor, 0, [[8, 16], [1, 8]])), 16)
    ldm = C.dL.n
    C.t.wait_ge(C.dL.h, ldm)
    C.t.matmul(pb[3][0:16, 0:8], sel16[:, :], gath[:, :], start=True, stop=False)
    tm = C.sT.inc(C.t.matmul(pb[3][0:16, 0:8], tri16[:, :], cnt16[:, :],
                             start=False, stop=True))
    C.v.wait_ge(C.sT.h, tm)
    vm = C.sV.inc(C.v.tensor_copy(prefall[:, :], pb[3][0:16, 0:8]))
    d_pref = C.scratch(f"{tag}_prefd", [16, 8])
    prefrow = C.sbuf("s6_prow", [1, 128], F32)
    C.wait(C.sy, (C.sV, vm))
    C.dL.inc(C.sy.dma_start(out=d_pref, in_=prefall[:, :]), 16)
    C.sy.wait_ge(C.dL.h, C.dL.n)
    C.dL.inc(C.sy.dma_start(out=prefrow[:, :],
                            in_=bass.AP(d_pref.tensor, 0, [[0, 1], [1, 128]])), 16)
    pfm = C.dL.n
    C.t.wait_ge(C.dL.h, pfm)
    for t in range(NT):
        C.t.wait_ge(C.sV.h, C.sV.n)
        C.t.matmul(pb[4][:, 0:8], trib[:, :], Mb[t][:, :], start=True, stop=False)
        h = C.t.matmul(pb[4][:, 0:8], onesf[:, :], prefrow[0:1, 8 * t:8 * (t + 1)],
                       start=False, stop=True)
        tm = C.sT.inc(h)
        C.v.wait_ge(C.sT.h, tm)
        vm = C.sV.inc(C.v.tensor_tensor(tmp8[:, :], pb[4][:, 0:8], Mb[t][:, :],
                                        ALU.mult))
        C.v.wait_ge(C.sV.h, vm)
        vm = C.sV.inc(C.v.tensor_reduce(estar[:, :], tmp8[:, :], AX.X, ALU.add))
        C.v.wait_ge(C.sV.h, vm)
        vm = C.sV.inc(C.v.tensor_scalar(estar[:, :], estar[:, :], 2048.0, None,
                                        ALU.is_lt))
        C.v.wait_ge(C.sV.h, vm)
        C.sV.inc(C.v.tensor_mul(coefc[:, t:t + 1], estar[:, :], gate[:, t:t + 1]))
    C.wait(C.sy, (C.sV, C.sV.n))
    C.dL.inc(C.sy.dma_start(out=d_coef, in_=coefc[:, :]), 16)
    C.sy.wait_ge(C.dL.h, C.dL.n)
    C.dL.inc(C.sy.dma_start(
        out=coefrow[:, :],
        in_=bass.AP(d_coef.tensor, 0, [[0, 1], [1, NT], [NT, 128]]),
        allow_slow_non_contiguous=True), 16)
    ldm2 = C.dL.n
    C.t.wait_ge(C.dL.h, ldm2)
    for qc in range(4):
        s_ = slice(512 * qc, 512 * (qc + 1))
        C.t.wait_ge(C.sV.h, C.sV.n)
        C.t.matmul(pb[5][:, :], onesf[:, :], coefrow[0:1, s_], start=True,
                   stop=True)
        for ei in range(ND):
            bank = pb[6] if ei % 2 == 0 else pb[7]
            if ei >= 2:
                C.t.wait_ge(C.sV.h, C.sV.n)
            for di in range(ND):
                h = C.t.matmul(bank[:, :], we[di][:, 128 * ei:128 * (ei + 1)],
                               xn[di][:, s_], start=(di == 0), stop=(di == ND - 1))
            tm = C.sT.inc(h)
            C.v.wait_ge(C.sT.h, tm)
            vm = C.sV.inc(C.v.tensor_scalar(tmpe[:, :], bank[:, :],
                                            bec[:, ei:ei + 1], None, ALU.add))
            C.v.wait_ge(C.sV.h, vm)
            vm = C.sV.inc(C.v.tensor_tensor(tmpe[:, :], tmpe[:, :], pb[5][:, :],
                                            ALU.mult))
            C.v.wait_ge(C.sV.h, vm)
            C.sV.inc(C.v.tensor_add(xT[ei][:, s_], xT[ei][:, s_], tmpe[:, :]))
    return (C.sV, C.sV.n)


def emit_s3(C, xT, mark, tag="s3"):
    pb = get_pb(C)
    onesb = C.sb["onesb"]
    onesf = C.sb["onesf"]
    xn = [C.sbuf(f"xn{i}", [128, T], BF16) for i in range(ND)]
    mark = emit_ln(C, xT, xn, mark)

    wq = C.inp("Wqkv", [D, 3 * D])
    wqkv = [C.sbuf(f"s3_w{i}", [128, 3 * D], BF16) for i in range(ND)]
    for i in range(ND):
        C.dG.inc(C.g.dma_start(out=wqkv[i][:, :],
                               in_=wq[128 * i:128 * (i + 1), :]), 16)
    bqkv = C.sbuf("s3_b", [128, 24], F32)
    C.dL.inc(C.sy.dma_start(out=bqkv[:, :], in_=C.inp("bqkv_c", [128, 24])), 16)
    bvr = C.sbuf("s3_bvr", [1, D], BF16)
    C.dG.inc(C.g.dma_start(out=bvr[:, :], in_=C.inp("bqkv_vr", [1, D])), 16)
    m0T = C.sbuf("s3_m0T", [128, 128], F32)
    m2T = C.sbuf("s3_m2T", [128, 128], F32)
    C.dL.inc(C.sy.dma_start(out=m0T[:, :], in_=C.inp("m0T", [128, 128])), 16)
    C.dL.inc(C.sy.dma_start(out=m2T[:, :], in_=C.inp("m2T", [128, 128])), 16)
    lnw = C.sbuf("s3_lnw", [128, ND], F32)
    lnb = C.sbuf("s3_lnb", [128, ND], F32)
    C.dL.inc(C.sy.dma_start(out=lnw[:, :], in_=C.inp("lnw_c", [128, ND])), 16)
    C.dL.inc(C.sy.dma_start(out=lnb[:, :], in_=C.inp("lnb_c", [128, ND])), 16)
    hidx = C.inp("haloidx", [128, ND], I32)
    for e in (C.v, C.a, C.t, C.g):
        e.wait_ge(C.dL.h, C.dL.n)
        e.wait_ge(C.dG.h, C.dG.n)

    d_tail = C.scratch(f"{tag}_tail", [D, 256], BF16)
    d_gath = C.scratch(f"{tag}_gath", [4 * D, 256], BF16)
    halo = [C.sbuf(f"s3_halo{i}", [128, 256], BF16) for i in range(ND)]
    C.wait(C.sy, mark)
    for i in range(ND):
        C.dL.inc(C.sy.dma_start(out=d_tail[128 * i:128 * (i + 1), :],
                                in_=xn[i][:, T - 256:T]), 16)
    C.g.wait_ge(C.dL.h, C.dL.n)
    cm = C.sC.inc(C.g.collective_compute(
        "AllGather", ALU.bypass, replica_groups=[[0, 1, 2, 3], [4, 5, 6, 7]],
        ins=[d_tail.opt()], outs=[d_gath.opt()]))
    for i in range(ND):
        C.v.memset(halo[i][:, :], 0.0)
    C.sV.inc(C.v.tensor_copy(halo[0][:, 0:1], halo[0][:, 0:1]))
    C.g.wait_ge(C.sC.h, cm)
    C.g.wait_ge(C.sV.h, C.sV.n)
    from concourse.bass_core_rust import IndirectOffsetOnAxis
    hxs = C.sbuf("s3_hidx", [128, ND], I32)
    C.dG.inc(C.g.dma_start(out=hxs[:, :], in_=hidx), 16)
    C.g.wait_ge(C.dG.h, C.dG.n)
    C.g.wait_ge(C.sV.h, C.sV.n)
    for i in range(ND):
        hh = C.g.indirect_dma_start(
            halo[i][:, :], None, d_gath,
            IndirectOffsetOnAxis(ap=hxs[:, i:i + 1], axis=0))
        C.dG.inc(hh, 16)
    hmask = C.sbuf("s3_hm", [128, 1], F32)
    C.dL.inc(C.sy.dma_start(out=hmask[:, :], in_=C.inp("hmask", [128, 1])), 16)
    C.v.wait_ge(C.dG.h, C.dG.n)
    C.v.wait_ge(C.dL.h, C.dL.n)
    for i in range(ND):
        h = C.v.tensor_scalar(halo[i][:, :], halo[i][:, :], hmask[:, :],
                              None, ALU.mult)
    C.sV.inc(h)
    for e in (C.v, C.a, C.t):
        e.wait_ge(C.dG.h, C.dG.n)
        e.wait_ge(C.sV.h, C.sV.n)

    kblk = [C.sbuf(f"s3_kb{j}", [128, D], BF16) for j in range(3)]
    vblk = [C.sbuf(f"s3_vb{j}", [128, D], BF16) for j in range(3)]
    qtt = C.sbuf("s3_qt", [128, D], BF16)
    attnT = C.sbuf("s3_at", [128, 3 * 128], BF16)
    oTt = C.sbuf("s3_ot", [128, D], BF16)
    osq = C.sbuf("s3_osq", [128, 128], BF16)
    orow = C.sbuf("s3_orow", [1, 2 * 128], F32)
    otmp = C.sbuf("s3_otmp", [128, 128], F32)

    def src_chunk(b, i):
        if b < 0:
            return halo[i][:, 128 * (b + 2):128 * (b + 3)]
        return xn[i][:, 128 * b:128 * (b + 1)]

    bmark = {}

    def bwait(r):
        if S3_COARSE:
            C.t.wait_ge(C.sV.h, C.sV.n)
        else:
            C.wait(C.t, bmark.get(r))

    def build_block(b, slot):
        for i in range(ND):
            bx = [0, 2][i % 2]
            bwait((bx, 0))
            for di in range(ND):
                h = C.t.matmul(pb[bx][:, 0:128],
                               wqkv[di][:, D + 128 * i:D + 128 * (i + 1)],
                               src_chunk(b, di), start=(di == 0), stop=(di == ND - 1))
            tm = C.sT.inc(h)
            C.v.wait_ge(C.sT.h, tm)
            vm = C.sV.inc(C.v.tensor_scalar(kblk[slot][:, 128 * i:128 * (i + 1)],
                                            pb[bx][:, 0:128],
                                            bqkv[:, 8 + i:9 + i], None, ALU.add))
            bmark[(bx, 0)] = (C.sV, vm)
        for eh in range(2):
            esl = slice(512 * eh, 512 * (eh + 1))
            bx = [1, 3][eh]
            bwait((bx, 0))
            for di in range(ND):
                C.t.matmul(pb[bx][:, :], src_chunk(b, di),
                           wqkv[di][:, 2 * D + 512 * eh:2 * D + 512 * (eh + 1)],
                           start=(di == 0), stop=False)
            h = C.t.matmul(pb[bx][:, :], onesb[:, :], bvr[:, esl],
                           start=False, stop=True)
            tm = C.sT.inc(h)
            C.v.wait_ge(C.sT.h, tm)
            vm = C.sV.inc(C.v.tensor_copy(vblk[slot][:, esl], pb[bx][:, :]))
            bmark[(bx, 0)] = (C.sV, vm)

    build_block(-2, 0)
    build_block(-1, 1)
    for t in range(NT):
        slot_t = (t + 2) % 3
        build_block(t, slot_t)
        qdone = None
        for i in range(ND):
            bx = [0, 2][i % 2]
            bwait((bx, 0))
            for di in range(ND):
                h = C.t.matmul(pb[bx][:, 0:128],
                               wqkv[di][:, 128 * i:128 * (i + 1)],
                               xn[di][:, 128 * t:128 * (t + 1)],
                               start=(di == 0), stop=(di == ND - 1))
            tm = C.sT.inc(h)
            C.v.wait_ge(C.sT.h, tm)
            qdone = C.sV.inc(C.v.tensor_scalar(qtt[:, 128 * i:128 * (i + 1)],
                                               pb[bx][:, 0:128],
                                               bqkv[:, i:i + 1], None, ALU.add))
            bmark[(bx, 0)] = (C.sV, qdone)
        C.t.wait_ge(C.sV.h, qdone)
        attndone = None
        for j in range(3):
            slot = (t + j) % 3
            bx = [1, 3][j % 2]
            bwait((bx, 0))
            for i in range(ND):
                h = C.t.matmul(pb[bx][:, 0:128],
                               kblk[slot][:, 128 * i:128 * (i + 1)],
                               qtt[:, 128 * i:128 * (i + 1)],
                               start=(i == 0), stop=(i == ND - 1))
            tm = C.sT.inc(h)
            C.v.wait_ge(C.sT.h, tm)
            if j == 0:
                attndone = C.sV.inc(C.v.tensor_tensor(
                    attnT[:, 128 * j:128 * (j + 1)],
                    pb[bx][:, 0:128], m0T[:, :], ALU.mult))
            elif j == 1:
                attndone = C.sV.inc(C.v.tensor_scalar(
                    attnT[:, 128 * j:128 * (j + 1)],
                    pb[bx][:, 0:128], 1.0 / 512.0, None, ALU.mult))
            else:
                attndone = C.sV.inc(C.v.tensor_tensor(
                    attnT[:, 128 * j:128 * (j + 1)],
                    pb[bx][:, 0:128], m2T[:, :], ALU.mult))
            bmark[(bx, 0)] = (C.sV, attndone)
        C.t.wait_ge(C.sV.h, attndone)
        otdone = None
        for i in range(ND):
            bk, off = 4 + (i % 2), 0
            bwait((bk, off))
            for j in range(3):
                slot = (t + j) % 3
                h = C.t.matmul(pb[bk][:, off:off + 128],
                               vblk[slot][:, 128 * i:128 * (i + 1)],
                               attnT[:, 128 * j:128 * (j + 1)],
                               start=(j == 0), stop=(j == 2))
            tm = C.sT.inc(h)
            C.v.wait_ge(C.sT.h, tm)
            otdone = C.sV.inc(C.v.tensor_copy(oTt[:, 128 * i:128 * (i + 1)],
                                              pb[bk][:, off:off + 128]))
            bmark[(bk, off)] = (C.sV, otdone)
        bwait((6, 0))
        C.t.wait_ge(C.sV.h, otdone)
        for i in range(ND):
            C.t.matmul(pb[6][0:1, 0:128], C.sb["onescb"][:, :],
                       oTt[:, 128 * i:128 * (i + 1)],
                       start=(i == 0), stop=(i == ND - 1))
        for i in range(ND):
            vm = C.sV.inc(C.v.tensor_mul(osq[:, :], oTt[:, 128 * i:128 * (i + 1)],
                                         oTt[:, 128 * i:128 * (i + 1)]))
            C.t.wait_ge(C.sV.h, vm)
            tm = C.sT.inc(C.t.matmul(pb[6][0:1, 128:256], C.sb["onescb"][:, :],
                                     osq[:, :], start=(i == 0), stop=(i == ND - 1)))
            C.v.wait_ge(C.sT.h, tm)
        C.v.wait_ge(C.sT.h, C.sT.n)
        vm = C.sV.inc(C.v.tensor_scalar(orow[0:1, 0:128], pb[6][0:1, 0:128],
                                        1.0 / 1024.0, None, ALU.mult))
        C.v.wait_ge(C.sV.h, vm)
        vm = C.sV.inc(C.v.tensor_mul(otmp[0:1, 0:128], orow[0:1, 0:128],
                                     orow[0:1, 0:128]))
        C.v.wait_ge(C.sV.h, vm)
        vm = C.sV.inc(C.v.tensor_scalar(orow[0:1, 128:256], pb[6][0:1, 128:256],
                                        1.0 / 1024.0, 1e-5, ALU.mult, ALU.add))
        bmark[(6, 0)] = (C.sV, vm)
        C.v.wait_ge(C.sV.h, vm)
        vm = C.sV.inc(C.v.tensor_sub(orow[0:1, 128:256], orow[0:1, 128:256],
                                     otmp[0:1, 0:128]))
        C.a.wait_ge(C.sV.h, vm)
        am = C.sA.inc(C.a.activation(orow[0:1, 128:256], orow[0:1, 128:256], AF.Ln))
        C.a.wait_ge(C.sA.h, am)
        am = C.sA.inc(C.a.activation(orow[0:1, 128:256], orow[0:1, 128:256],
                                     AF.Exp, scale=-0.5))
        C.v.wait_ge(C.sA.h, am)
        vm = C.sV.inc(C.v.scalar_tensor_tensor(orow[0:1, 0:128], orow[0:1, 0:128],
                                               -1.0, orow[0:1, 128:256],
                                               ALU.mult, ALU.mult))
        C.t.wait_ge(C.sV.h, vm)
        bwait((7, 0))
        C.t.matmul(pb[7][:, 0:128], onesf[:, :], orow[0:1, 128:256],
                   start=True, stop=True)
        tm = C.sT.inc(C.t.matmul(pb[7][:, 128:256], onesf[:, :], orow[0:1, 0:128],
                                 start=True, stop=True))
        C.v.wait_ge(C.sT.h, tm)
        lastv = None
        for i in range(ND):
            tsl = slice(128 * t, 128 * (t + 1))
            vm = C.sV.inc(C.v.tensor_tensor(otmp[:, :], oTt[:, 128 * i:128 * (i + 1)],
                                            pb[7][:, 0:128], ALU.mult))
            C.v.wait_ge(C.sV.h, vm)
            vm = C.sV.inc(C.v.tensor_tensor(otmp[:, :], otmp[:, :],
                                            pb[7][:, 128:256], ALU.add))
            C.v.wait_ge(C.sV.h, vm)
            vm = C.sV.inc(C.v.tensor_scalar(otmp[:, :], otmp[:, :],
                                            lnw[:, i:i + 1], lnb[:, i:i + 1],
                                            ALU.mult, ALU.add))
            C.v.wait_ge(C.sV.h, vm)
            lastv = C.sV.inc(C.v.tensor_add(xT[i][:, tsl], xT[i][:, tsl],
                                            otmp[:, :]))
        bmark[(7, 0)] = (C.sV, lastv)
    return (C.sV, C.sV.n)


def _register():
    EMITTERS[2] = lambda C, xT, mark: emit_s2(C, xT, mark, "s2")
    EMITTERS[7] = lambda C, xT, mark: emit_s7(C, xT, mark)
    EMITTERS[6] = lambda C, xT, mark: emit_s6(C, xT, mark, "s6")
    EMITTERS[3] = lambda C, xT, mark: emit_s3(C, xT, mark)
    EMITTERS[8] = lambda C, xT, mark: emit_s6(C, xT, mark, "s8")
    EMITTERS[4] = lambda C, xT, mark: emit_s2(C, xT, mark, "s4")


_register()


def emit_s5(C, xT, mark, tag="s5"):
    pb = get_pb(C)
    xn = [C.sbuf(f"xn{i}", [128, T], BF16) for i in range(ND)]
    g1 = C.sbuf("s5_g1", [128, ND], F32)
    be1 = C.sbuf("s5_be1", [128, ND], F32)
    C.dL.inc(C.sy.dma_start(out=g1[:, :], in_=C.inp("g1c", [128, ND])), 16)
    C.dL.inc(C.sy.dma_start(out=be1[:, :], in_=C.inp("be1c", [128, ND])), 16)
    C.v.wait_ge(C.dL.h, C.dL.n)
    mark = emit_ln(C, xT, xn, mark)
    d_xn = C.scratch(f"{tag}_xn", [D, T], BF16)
    C.wait(C.sy, mark)
    for i in range(ND):
        C.dL.inc(C.sy.dma_start(out=d_xn[128 * i:128 * (i + 1), :],
                                in_=xn[i][:, :]), 16)
    spillm = C.dL.n
    for e in (C.v, C.a, C.t):
        e.wait_ge(C.dL.h, spillm)
    mark = emit_ln(C, xn, xn, mark, gamma="s5_g1", beta="s5_be1",
                   eps=1e-6, f32src=False)
    ln1 = xn

    pool = [C.sbuf(f"s5_wp{i}", [128, D], BF16) for i in range(2 * ND)]
    qT = [C.sbuf(f"s5_qT{i}", [128, T], BF16) for i in range(ND)]
    kt = C.sbuf("s5_kt", [128, D], BF16)
    vt = C.sbuf("s5_vt", [128, D], BF16)
    kvsS = C.sbuf("s5_kvs", [64, 1040], F32)
    kvb = C.sbuf("s5_kvb", [128, 1040], BF16)
    rd = C.sbuf("s5_rd", [128, 1], F32)
    b1c = C.sbuf("s5_b1c", [128, 32], F32)
    b2c = C.sbuf("s5_b2c", [128, ND], F32)
    C.dL.inc(C.sy.dma_start(out=b1c[:, :], in_=C.inp("b1c", [128, 32])), 16)
    C.dL.inc(C.sy.dma_start(out=b2c[:, :], in_=C.inp("b2c", [128, ND])), 16)
    d_kv = C.scratch(f"{tag}_kv", [64, 1040])
    d_kvr = C.scratch(f"{tag}_kvr", [64, 1040])

    wqf = C.inp("wqf", [D, D])
    for i in range(ND):
        C.dG.inc(C.g.dma_start(out=pool[i][:, :],
                               in_=wqf[128 * i:128 * (i + 1), :]), 16)
    C.wait(C.t, mark)
    C.t.wait_ge(C.dG.h, C.dG.n)
    C.t.wait_ge(C.sA.h, C.sA.n)
    C.t.wait_ge(C.sV.h, C.sV.n)
    amk = {}
    for fi in range(ND):
        for qc in range(4):
            bx = qc % 2
            bank = pb[bx]
            s_ = slice(512 * qc, 512 * (qc + 1))
            C.wait(C.t, amk.get(bx))
            for di in range(ND):
                h = C.t.matmul(bank[:, :], pool[di][:, 128 * fi:128 * (fi + 1)],
                               ln1[di][:, s_], start=(di == 0), stop=(di == ND - 1))
            tm = C.sT.inc(h)
            C.a.wait_ge(C.sT.h, tm)
            am = C.sA.inc(C.a.activation(qT[fi][:, s_], bank[:, :], AF.Relu))
            amk[bx] = (C.sA, am)
    C.v.wait_ge(C.sA.h, C.sA.n)
    for fi in range(ND):
        h = C.v.tensor_scalar(qT[fi][:, :], qT[fi][:, :], 0.001, None, ALU.add)
    qpm = C.sV.inc(h)

    wkf = C.inp("wkf", [D, D])
    wvf = C.inp("wvf", [D, D])
    C.g.wait_ge(C.sT.h, C.sT.n)
    for i in range(ND):
        C.dG.inc(C.g.dma_start(out=pool[i][:, :],
                               in_=wkf[128 * i:128 * (i + 1), :]), 16)
        C.dG.inc(C.g.dma_start(out=pool[ND + i][:, :],
                               in_=wvf[128 * i:128 * (i + 1), :]), 16)
    C.t.wait_ge(C.dG.h, C.dG.n)
    C.t.wait_ge(C.sV.h, qpm)
    lastkv = None
    pmk = {}
    for lt in range(NT):
        tsl = slice(128 * lt, 128 * (lt + 1))
        for eh in range(2):
            esl = slice(512 * eh, 512 * (eh + 1))
            C.wait(C.t, pmk.get(0))
            for di in range(ND):
                h = C.t.matmul(pb[0][:, :], ln1[di][:, tsl], pool[di][:, esl],
                               start=(di == 0), stop=(di == ND - 1))
            tm = C.sT.inc(h)
            C.a.wait_ge(C.sT.h, tm)
            am = C.sA.inc(C.a.activation(kt[:, esl], pb[0][:, :], AF.Relu))
            pmk[0] = (C.sA, am)
            C.wait(C.t, pmk.get(1))
            for di in range(ND):
                h = C.t.matmul(pb[1][:, :], ln1[di][:, tsl],
                               pool[ND + di][:, esl],
                               start=(di == 0), stop=(di == ND - 1))
            tm = C.sT.inc(h)
            C.v.wait_ge(C.sT.h, tm)
            vm = C.sV.inc(C.v.tensor_copy(vt[:, esl], pb[1][:, :]))
            pmk[1] = (C.sV, vm)
        C.v.wait_ge(C.sA.h, C.sA.n)
        vm = C.sV.inc(C.v.tensor_scalar(kt[:, :], kt[:, :], 0.001, None, ALU.add))
        C.t.wait_ge(C.sV.h, vm)
        for hh in range(16):
            bank = pb[2 + hh // 8]
            hs = slice(64 * hh, 64 * (hh + 1))
            lastkv = C.t.matmul(bank[0:64, 64 * (hh % 8):64 * (hh % 8) + 64],
                                kt[:, hs], vt[:, hs],
                                start=(lt == 0 and hh % 8 == 0),
                                stop=(lt == NT - 1 and hh % 8 == 7))
            lastks = C.t.matmul(pb[4][0:64, hh:hh + 1], kt[:, hs],
                                C.sb["onescb"][:, :],
                                start=(lt == 0 and hh == 0),
                                stop=(lt == NT - 1 and hh == 15))
    tm = C.sT.inc(lastks)
    C.sT.inc(lastkv)
    C.v.wait_ge(C.sT.h, C.sT.n)
    C.v.tensor_copy(kvsS[:, 0:512], pb[2][0:64, 0:512])
    C.v.tensor_copy(kvsS[:, 512:1024], pb[3][0:64, 0:512])
    vm = C.sV.inc(C.v.tensor_copy(kvsS[:, 1024:1040], pb[4][0:64, 0:16]))
    C.wait(C.sy, (C.sV, vm))
    C.dL.inc(C.sy.dma_start(out=d_kv, in_=kvsS[:, :]), 16)
    C.g.wait_ge(C.dL.h, C.dL.n)
    cm = C.sC.inc(C.g.collective_compute(
        "AllReduce", ALU.add, replica_groups=[[0, 1, 2, 3], [4, 5, 6, 7]],
        ins=[d_kv.opt()], outs=[d_kvr.opt()]))
    C.sy.wait_ge(C.sC.h, cm)
    C.dL.inc(C.sy.dma_start(out=kvsS[:, :], in_=d_kvr), 16)
    C.v.wait_ge(C.dL.h, C.dL.n)
    C.v.tensor_copy(kvb[0:64, :], kvsS[:, :])
    vm = C.sV.inc(C.v.tensor_copy(kvb[64:128, :], kvsS[:, :]))

    C.t.wait_ge(C.sV.h, vm)
    cmk = {}
    it = 0
    for hh in range(16):
        fi = hh // 2
        r0 = 64 * (hh % 2)
        for lt in range(NT):
            tsl = slice(128 * lt, 128 * (lt + 1))
            ba, bt = (5, 6) if it % 2 == 0 else (0, 1)
            it += 1
            C.wait(C.t, cmk.get(ba))
            C.t.matmul(pb[ba][:, 0:64], qT[fi][r0:r0 + 64, tsl],
                       kvb[r0:r0 + 64, 64 * hh:64 * hh + 64],
                       start=True, stop=False)
            h = C.t.matmul(pb[ba][:, 64:65], qT[fi][r0:r0 + 64, tsl],
                           kvb[r0:r0 + 64, 1024 + hh:1025 + hh],
                           start=False, stop=True)
            tm = C.sT.inc(h)
            C.v.wait_ge(C.sT.h, tm)
            vm = C.sV.inc(C.v.reciprocal(rd[:, :], pb[ba][:, 64:65]))
            C.v.wait_ge(C.sV.h, vm)
            vm = C.sV.inc(C.v.tensor_scalar(kt[:, 0:64], pb[ba][:, 0:64],
                                            rd[:, :], None, ALU.mult))
            cmk[ba] = (C.sV, vm)
            C.wait(C.t, cmk.get(bt))
            C.t.wait_ge(C.sV.h, vm)
            tm = C.sT.inc(C.t.transpose(pb[bt][:, 0:64].bitcast(BF16)[0:64, 0:128],
                                        kt[:, 0:64], C.sb["idb"][:, :]))
            C.v.wait_ge(C.sT.h, tm)
            vm = C.sV.inc(C.v.tensor_copy(qT[fi][r0:r0 + 64, tsl],
                                          pb[bt][:, 0:64].bitcast(BF16)[0:64, 0:128]))
            cmk[bt] = (C.sV, vm)
    attm = (C.sV, C.sV.n)

    wof = C.inp("wof", [D, D])
    C.g.wait_ge(C.sT.h, C.sT.n)
    for i in range(ND):
        C.dG.inc(C.g.dma_start(out=pool[i][:, :],
                               in_=wof[128 * i:128 * (i + 1), :]), 16)
    C.wait(C.sy, attm)
    for i in range(ND):
        C.dL.inc(C.sy.dma_start(out=xn[i][:, :],
                                in_=d_xn[128 * i:128 * (i + 1), :]), 16)
    C.wait(C.t, attm)
    C.t.wait_ge(C.dG.h, C.dG.n)
    for e in (C.v, C.a, C.t):
        e.wait_ge(C.dL.h, C.dL.n)
    dmk = {}
    for ei in range(ND):
        for qc in range(4):
            bx = qc % 2
            bank = pb[bx]
            s_ = slice(512 * qc, 512 * (qc + 1))
            C.wait(C.t, dmk.get(bx))
            for di in range(ND):
                h = C.t.matmul(bank[:, :], pool[di][:, 128 * ei:128 * (ei + 1)],
                               qT[di][:, s_], start=(di == 0), stop=(di == ND - 1))
            tm = C.sT.inc(h)
            C.v.wait_ge(C.sT.h, tm)
            vm = C.sV.inc(C.v.tensor_add(xn[ei][:, s_], xn[ei][:, s_],
                                         bank[:, :]))
            dmk[bx] = (C.sV, vm)
            C.v.wait_ge(C.sV.h, vm)
            C.sV.inc(C.v.tensor_add(xT[ei][:, s_], xT[ei][:, s_],
                                    xn[ei][:, s_]))
    mark = (C.sV, C.sV.n)

    mark = emit_ln(C, xn, xn, mark, gamma="s5_g1", beta="s5_be1",
                   eps=1e-6, f32src=False)

    w1i = C.inp("w1i", [32, 128, 8, 128])
    w2i = C.inp("w2i", [8, 128, 32, 128])
    w14 = [C.sbuf(f"s5_w14_{j}", [128, 4096], BF16, alias="scrA",
                  off=8192 * j) for j in range(2)]
    C.wait(C.t, mark)
    C.wait(C.g, mark)
    wld = [None, None]
    fmk = {}
    ktm = {}
    for tc in range(4):
        s_ = slice(512 * tc, 512 * (tc + 1))
        for g4 in range(8):
            j = g4 % 2
            C.wait(C.g, wld[j])
            C.dG.inc(C.g.dma_start(
                out=w14[j][:, :],
                in_=bass.AP(w1i.tensor, 4 * g4 * 128 * 1024,
                            [[1024, 128], [128 * 1024, 4], [1, 1024]])), 16)
            ldw = C.dG.n
            C.t.wait_ge(C.dG.h, ldw)
            for q4 in range(4):
                fp = 4 * g4 + q4
                h1slot = qT[fp // 4][:, 512 * (fp % 4):512 * (fp % 4 + 1)]
                bx = fp % 2
                bank = pb[bx]
                C.wait(C.t, fmk.get(bx))
                for di in range(ND):
                    h = C.t.matmul(bank[:, :],
                                   w14[j][:, 1024 * q4 + 128 * di:
                                          1024 * q4 + 128 * (di + 1)],
                                   ln1[di][:, s_], start=(di == 0),
                                   stop=(di == ND - 1))
                tm = C.sT.inc(h)
                C.a.wait_ge(C.sT.h, tm)
                am = C.sA.inc(C.a.activation(h1slot, bank[:, :], AF.Gelu,
                                             bias=b1c[:, fp:fp + 1]))
                fmk[bx] = (C.sA, am)
            wld[j] = (C.sT, C.sT.n)
        C.wait(C.t, fmk.get(0))
        C.wait(C.t, fmk.get(1))
        for ei in range(ND):
            j = ei % 2
            C.wait(C.g, wld[j])
            C.dG.inc(C.g.dma_start(
                out=w14[j][:, :],
                in_=bass.AP(w2i.tensor, ei * 128 * 4096,
                            [[4096, 128], [1, 4096]])), 16)
            ldw = C.dG.n
            C.t.wait_ge(C.dG.h, ldw)
            bx = ei % 2
            bank = pb[bx]
            ks = kt[:, 512 * bx:512 * (bx + 1)]
            C.wait(C.t, fmk.get(bx))
            for fp in range(32):
                h1slot = qT[fp // 4][:, 512 * (fp % 4):512 * (fp % 4 + 1)]
                h = C.t.matmul(bank[:, :], w14[j][:, 128 * fp:128 * (fp + 1)],
                               h1slot, start=(fp == 0), stop=(fp == 31))
            tm = C.sT.inc(h)
            wld[j] = (C.sT, C.sT.n)
            C.a.wait_ge(C.sT.h, tm)
            C.wait(C.a, ktm.get(bx))
            am = C.sA.inc(C.a.activation(ks, bank[:, :], AF.Gelu,
                                         bias=b2c[:, ei:ei + 1]))
            fmk[bx] = (C.sA, am)
            C.v.wait_ge(C.sA.h, am)
            vm = C.sV.inc(C.v.tensor_add(xT[ei][:, s_], xT[ei][:, s_], ks))
            ktm[bx] = (C.sV, vm)
    return (C.sV, C.sV.n)


def _register():
    EMITTERS[2] = lambda C, xT, mark: emit_s2(C, xT, mark, "s2")
    EMITTERS[7] = lambda C, xT, mark: emit_s7(C, xT, mark)
    EMITTERS[6] = lambda C, xT, mark: emit_s6(C, xT, mark, "s6")
    EMITTERS[3] = lambda C, xT, mark: emit_s3(C, xT, mark)
    EMITTERS[8] = lambda C, xT, mark: emit_s6(C, xT, mark, "s8")
    EMITTERS[4] = lambda C, xT, mark: emit_s2(C, xT, mark, "s4")
    EMITTERS[5] = lambda C, xT, mark: emit_s5(C, xT, mark)
    EMITTERS[1] = lambda C, xT, mark: emit_s1(C, xT, mark)


_register()


S1_DEBUG = False
S3_COARSE = False
S3_MONO = True


def emit_s1(C, xT, mark, tag="s1"):
    pb = get_pb(C)
    onesf = C.sb["onesf"]
    onesb = C.sb["onesb"]
    xn = [C.sbuf(f"xn{i}", [128, T], BF16) for i in range(ND)]
    mark = emit_ln(C, xT, xn, mark)

    pibuf = [C.sbuf(f"s1_pi{i}", [128, T], BF16) for i in range(ND)]
    qT = [C.sbuf(f"s1_qT{i}", [128, T], BF16) for i in range(ND)]
    pool = [C.sbuf(f"s1_wp{i}", [128, D], BF16) for i in range(ND)]
    sc2 = C.sbuf("s1_sc2", [128, T], BF16)
    binq = C.sbuf("s1_bq", [128, ND], F32)
    bink = C.sbuf("s1_bk", [128, ND], F32)
    bout = C.sbuf("s1_bo", [128, ND], F32)
    binvr = C.sbuf("s1_bvr", [1, D], BF16)
    C.dL.inc(C.sy.dma_start(out=binq[:, :], in_=C.inp("binq_c", [128, ND])), 16)
    C.dL.inc(C.sy.dma_start(out=bink[:, :], in_=C.inp("bink_c", [128, ND])), 16)
    C.dL.inc(C.sy.dma_start(out=bout[:, :], in_=C.inp("bout_c", [128, ND])), 16)
    C.dG.inc(C.g.dma_start(out=binvr[:, :], in_=C.inp("binv_r", [1, D])), 16)

    d_kcon = C.scratch(f"{tag}_kcon", [D, T], BF16)
    d_vcon = C.scratch(f"{tag}_vcon", [T, 1040], BF16)
    d_kg = C.scratch(f"{tag}_kg", [4 * D, T], BF16)
    d_vg = C.scratch(f"{tag}_vg", [4 * T, 1040], BF16)
    d_expE = C.scratch(f"{tag}_expE", [16 * 16000], BF16)
    d_expb = C.scratch(f"{tag}_expb", [64 * 128 * 31 * 128], BF16)
    epad = C.inp("epad", [16, 16000])

    C.wait(C.v, mark)
    for i in range(ND):
        h = C.v.tensor_copy(
            pibuf[i][:, :].rearrange("p (r m) -> p r m", r=16),
            xn[i][:, :].rearrange("p (m r) -> p r m", r=16))
    vm = C.sV.inc(h)
    C.v.wait_ge(C.sV.h, vm)
    for i in range(ND):
        h = C.v.tensor_copy(
            xn[i][:, :].rearrange("p (r u) -> p r u", r=16),
            pibuf[i][:, :].rearrange("p (r u) -> p r u", r=16)[:, :, ::-1])
    phim = C.sV.inc(h)
    phi = xn
    if S1_DEBUG:
        C.wait(C.sy, (C.sV, phim))
        C.dS.inc(C.sy.dma_start(out=C.out("dbg_phi0", [128, T], BF16),
                                in_=xn[0][:, :]), 16)
        C.dS.inc(C.sy.dma_start(out=C.out("dbg_pi0", [128, T], BF16),
                                in_=pibuf[0][:, :]), 16)
        C.sy.wait_ge(C.dS.h, C.dS.n)

    ex1 = sc2[:, :].bitcast(F32)[:, 0:1000]
    ex2 = qT[0][:, 0:1000]
    for hf in range(2):
        epflat = bass.AP(epad.tensor, 1000 * hf, [[2000, 128], [1, 1000]])
        C.sy.wait_ge(C.dL.h, C.dL.n)
        C.wait(C.sy, (C.sA, C.sA.n))
        C.dL.inc(C.sy.dma_start(out=ex1, in_=epflat), 16)
        exm = C.dL.n
        C.a.wait_ge(C.dL.h, exm)
        am = C.sA.inc(C.a.activation(ex2, ex1, AF.Exp))
        C.wait(C.sy, (C.sA, am))
        C.dL.inc(C.sy.dma_start(
            out=bass.AP(d_expE.tensor, 1000 * hf, [[2000, 128], [1, 1000]]),
            in_=ex2), 16)
    C.sy.wait_ge(C.dL.h, C.dL.n)
    p4a_mark = (C.sA, C.sA.n)
    p4a_dma = (C.dL, C.dL.n)
    for kc in range(64):
        cbk, rk = kc // 16, kc % 16
        for g in range(4):
            nd31 = 8 if g < 3 else 7
            src = bass.AP(d_expE.tensor,
                          16000 * rk + 128 * cbk + 512 * (8 * g),
                          [[1, 128], [512, nd31], [1, 128]])
            dst = bass.AP(d_expb.tensor,
                          kc * 128 * 3968 + 128 * (8 * g),
                          [[3968, 128], [128, nd31], [1, 128]])
            C.dL.inc(C.sy.dma_start(out=dst, in_=src), 16)
    prepm = C.dL.n

    wqd = C.inp("Winq", [D, D])
    for i in range(ND):
        C.dG.inc(C.g.dma_start(out=pool[i][:, :],
                               in_=wqd[128 * i:128 * (i + 1), :]), 16)
    C.wait(C.t, (C.sV, phim))
    C.t.wait_ge(C.dG.h, C.dG.n)
    C.v.wait_ge(C.sA.h, p4a_mark[1])
    C.v.wait_ge(C.dL.h, p4a_dma[1])
    C.t.wait_ge(C.sV.h, C.sV.n)
    C.t.wait_ge(C.sA.h, C.sA.n)
    pmk = {}
    for fi in range(ND):
        for qc in range(4):
            bx = qc % 2
            bank = pb[bx]
            s_ = slice(512 * qc, 512 * (qc + 1))
            C.wait(C.t, pmk.get(bx))
            for di in range(ND):
                h = C.t.matmul(bank[:, :], pool[di][:, 128 * fi:128 * (fi + 1)],
                               phi[di][:, s_], start=(di == 0), stop=(di == ND - 1))
            tm = C.sT.inc(h)
            C.v.wait_ge(C.sT.h, tm)
            vm = C.sV.inc(C.v.tensor_scalar(qT[fi][:, s_], bank[:, :],
                                            binq[:, fi:fi + 1], 0.125,
                                            ALU.add, ALU.mult))
            pmk[bx] = (C.sV, vm)
    wkd = C.inp("Wink", [D, D])
    C.g.wait_ge(C.sT.h, C.sT.n)
    for i in range(ND):
        C.dG.inc(C.g.dma_start(out=pool[i][:, :],
                               in_=wkd[128 * i:128 * (i + 1), :]), 16)
    C.t.wait_ge(C.dG.h, C.dG.n)
    stk = None
    for fi in range(ND):
        C.wait(C.v, stk)
        for qc in range(4):
            bx = qc % 2
            bank = pb[bx]
            s_ = slice(512 * qc, 512 * (qc + 1))
            C.wait(C.t, pmk.get(bx))
            for di in range(ND):
                h = C.t.matmul(bank[:, :], pool[di][:, 128 * fi:128 * (fi + 1)],
                               pibuf[di][:, s_], start=(di == 0), stop=(di == ND - 1))
            tm = C.sT.inc(h)
            C.v.wait_ge(C.sT.h, tm)
            vm = C.sV.inc(C.v.tensor_scalar(sc2[:, s_], bank[:, :],
                                            bink[:, fi:fi + 1], None, ALU.add))
            pmk[bx] = (C.sV, vm)
        C.wait(C.sy, (C.sV, C.sV.n))
        C.dL.inc(C.sy.dma_start(out=d_kcon[128 * fi:128 * (fi + 1), :],
                                in_=sc2[:, :]), 16)
        stk = (C.dL, C.dL.n)
    wvd = C.inp("Winv", [D, D])
    C.g.wait_ge(C.sT.h, C.sT.n)
    for i in range(ND):
        C.dG.inc(C.g.dma_start(out=pool[i][:, :],
                               in_=wvd[128 * i:128 * (i + 1), :]), 16)
    C.t.wait_ge(C.dG.h, C.dG.n)
    C.v.wait_ge(C.dL.h, C.dL.n)
    vo = C.sV.inc(C.v.memset(sc2[:, :], 1.0))
    C.t.wait_ge(C.sV.h, vo)
    stv = None
    for hh in range(16):
        C.wait(C.v, stv)
        for ti in range(NT):
            bx = ti % 2
            bank = pb[bx]
            tsl = slice(128 * ti, 128 * (ti + 1))
            C.wait(C.t, pmk.get(bx))
            for di in range(ND):
                C.t.matmul(bank[:, 0:64], pibuf[di][:, tsl],
                           pool[di][:, 64 * hh:64 * (hh + 1)],
                           start=(di == 0), stop=False)
            h = C.t.matmul(bank[:, 0:64], onesb[:, :],
                           binvr[:, 64 * hh:64 * (hh + 1)], start=False, stop=True)
            tm = C.sT.inc(h)
            C.v.wait_ge(C.sT.h, tm)
            vm = C.sV.inc(C.v.tensor_copy(sc2[:, 65 * ti:65 * ti + 64],
                                          bank[:, 0:64]))
            pmk[bx] = (C.sV, vm)
        C.wait(C.sy, (C.sV, C.sV.n))
        C.dL.inc(C.sy.dma_start(
            out=bass.AP(d_vcon.tensor, 1040 * 128 * hh, [[1040, 128], [1, 1040]]),
            in_=sc2[:, 0:1040]), 16)
        stv = (C.dL, C.dL.n)
    conm = C.dL.n

    C.g.wait_ge(C.dL.h, conm)
    cm1 = C.sC.inc(C.g.collective_compute(
        "AllGather", ALU.bypass, replica_groups=[[0, 1, 2, 3], [4, 5, 6, 7]],
        ins=[d_kcon.opt()], outs=[d_kg.opt()]))
    cm2 = C.sC.inc(C.g.collective_compute(
        "AllGather", ALU.bypass, replica_groups=[[0, 1, 2, 3], [4, 5, 6, 7]],
        ins=[d_vcon.opt()], outs=[d_vg.opt()]))
    for e in (C.sy, C.t, C.v, C.a, C.g):
        e.wait_ge(C.sC.h, C.sC.n)
    C.sy.wait_ge(C.dL.h, prepm)

    NEB = 4
    eb = [C.sbuf(f"s1_e{j}", [128, 2048], BF16, alias="scrA",
                 off=4096 * j) for j in range(NEB)]
    exps2 = [xn[0], sc2]
    rrow = binvr[0:1, :].bitcast(F32)[0:1, 0:512]
    aon = pibuf
    pSh = [pb[8][:, 0:1024], pb[8][:, 1024:2048]]
    pA = [pb[4 + qc] for qc in range(4)]
    ebm = [None] * NEB
    ebl = [0] * NEB
    avx = [None, None]
    avk = [None, None]
    avv = [None, None]
    exph = [None, None]
    scm = {}

    def emit_loads(hh, kc):
        cbk, rk = kc // 16, kc % 16
        j4 = (kc // 4) % 2
        jv = (kc // 16) % 2
        jb = kc % NEB
        if rk == 0:
            C.wait(C.g, avv[jv])
            C.dG.inc(C.g.dma_start(
                out=xn[3 + jv][:, 0:1040],
                in_=d_vg[2048 * cbk + 128 * hh:2048 * cbk + 128 * (hh + 1), :]), 16)
        if rk % 4 == 0:
            C.wait(C.g, avk[j4])
            r0 = 64 * (hh % 2)
            C.dG.inc(C.g.dma_start(
                out=xn[1 + j4][r0:r0 + 64, 0:512],
                in_=d_kg[1024 * cbk + 64 * hh:1024 * cbk + 64 * (hh + 1),
                         128 * rk:128 * (rk + 4)]), 16)
        C.wait(C.sy, ebm[jb])
        C.dL.inc(C.sy.dma_start(
            out=eb[jb][:, :],
            in_=bass.AP(d_expb.tensor, kc * 128 * 3968 + 128 * (15 - hh),
                        [[3968, 128], [1, 2048]])), 16)
        ebl[jb] = C.dL.n
        return C.dL.n, C.dG.n

    def emit_scores(hh, kc, ldg):
        fi, r0 = hh // 2, 64 * (hh % 2)
        rk = kc % 16
        j4 = (kc // 4) % 2
        lhs = xn[1 + j4][r0:r0 + 64, 128 * (rk % 4):128 * (rk % 4) + 128]
        C.t.wait_ge(C.dG.h, ldg)
        C.wait(C.t, exph[0])
        for qc in (0, 1):
            h = C.t.matmul(pb[qc][:, :], lhs, qT[fi][r0:r0 + 64,
                           512 * qc:512 * (qc + 1)], start=True, stop=True)
        tmA = C.sT.inc(h)
        C.wait(C.t, exph[1])
        for qc in (2, 3):
            h = C.t.matmul(pb[qc][:, :], lhs, qT[fi][r0:r0 + 64,
                           512 * qc:512 * (qc + 1)], start=True, stop=True)
        scm[kc] = (tmA, C.sT.inc(h))

    ld0 = emit_loads(0, 0)
    emit_scores(0, 0, ld0[1])
    lds = {1: None}
    for hh in range(16):
        fi, r0 = hh // 2, 64 * (hh % 2)
        for kc in range(64):
            jv = (kc // 16) % 2
            je = kc % 2
            jb = kc % NEB
            rk = kc % 16
            nxt = (hh, kc + 1) if kc + 1 < 64 else ((hh + 1, 0) if hh + 1 < 16 else None)
            if nxt is not None:
                ldn = emit_loads(*nxt)
            tmA, tmB = scm.pop(kc)
            qmarks = []
            for qc in range(4):
                C.a.wait_ge(C.sT.h, tmA if qc < 2 else tmB)
                if qc == 0:
                    C.wait(C.a, avx[je])
                am = C.sA.inc(C.a.activation(
                    exps2[je][:, 512 * qc:512 * (qc + 1)], pb[qc][:, :], AF.Exp))
                exph[qc // 2] = (C.sA, am)
                C.v.wait_ge(C.sA.h, am)
                if qc == 0:
                    C.v.wait_ge(C.dL.h, ebl[jb])
                    C.wait(C.v, avx[je])
                vm = C.sV.inc(C.v.tensor_mul(
                    exps2[je][:, 512 * qc:512 * (qc + 1)],
                    exps2[je][:, 512 * qc:512 * (qc + 1)],
                    eb[jb][:, 512 * qc:512 * (qc + 1)]))
                qmarks.append(vm)
            ebm[jb] = (C.sV, qmarks[-1])
            if nxt is not None and nxt[1] != 0:
                emit_scores(nxt[0], nxt[1], ldn[1])
            for qc in range(4):
                C.t.wait_ge(C.sV.h, qmarks[qc])
                h = C.t.matmul(pA[qc][0:65, :],
                               xn[3 + jv][:, 65 * rk:65 * rk + 65],
                               exps2[je][:, 512 * qc:512 * (qc + 1)],
                               start=(kc == 0), stop=(kc == 63))
            mk = (C.sT, C.sT.inc(h))
            avx[je] = mk
            if rk == 15:
                avv[jv] = mk
            if rk % 4 == 3:
                avk[(kc // 4) % 2] = mk
        for qc in range(4):
            s_ = slice(512 * qc, 512 * (qc + 1))
            C.v.wait_ge(C.sT.h, C.sT.n)
            vm = C.sV.inc(C.v.tensor_copy(rrow[:, :], pA[qc][64:65, :]))
            C.v.wait_ge(C.sV.h, vm)
            vm = C.sV.inc(C.v.reciprocal(rrow[:, :], rrow[:, :]))
            C.t.wait_ge(C.sV.h, vm)
            C.wait(C.t, exph[0])
            C.wait(C.t, exph[1])
            tm = C.sT.inc(C.t.matmul(pb[0][0:64, 0:512], onesf[:, 0:64],
                                     rrow[:, :], start=True, stop=True))
            C.v.wait_ge(C.sT.h, tm)
            vm = C.sV.inc(C.v.tensor_copy(aon[fi][r0:r0 + 64, s_],
                                          pA[qc][0:64, :]))
            C.v.wait_ge(C.sV.h, vm)
            C.sV.inc(C.v.tensor_tensor(aon[fi][r0:r0 + 64, s_],
                                       aon[fi][r0:r0 + 64, s_],
                                       pb[0][0:64, 0:512], ALU.mult))
        exph = [None, None]
        C.t.wait_ge(C.sV.h, C.sV.n)
        if hh + 1 < 16:
            emit_scores(hh + 1, 0, C.dG.n)

    if S1_DEBUG:
        C.wait(C.sy, (C.sV, C.sV.n))
        C.dS.inc(C.sy.dma_start(out=C.out("dbg_qT0", [128, T], BF16), in_=qT[0][:, :]), 16)
        C.dS.inc(C.sy.dma_start(out=C.out("dbg_aon0", [128, T], BF16), in_=aon[0][:, :]), 16)
        C.sy.wait_ge(C.dS.h, C.dS.n)
    wod = C.inp("Wout", [D, D])
    C.g.wait_ge(C.sT.h, C.sT.n)
    for i in range(ND):
        C.dG.inc(C.g.dma_start(out=pool[i][:, :],
                               in_=wod[128 * i:128 * (i + 1), :]), 16)
    C.t.wait_ge(C.dG.h, C.dG.n)
    C.t.wait_ge(C.sV.h, C.sV.n)
    ophi = eb[0]
    opi = eb[1]
    for ei in range(ND):
        for qc in range(4):
            bank = pb[0] if qc % 2 == 0 else pb[1]
            s_ = slice(512 * qc, 512 * (qc + 1))
            if not (ei == 0 and qc == 0):
                C.t.wait_ge(C.sV.h, C.sV.n)
            for di in range(ND):
                h = C.t.matmul(bank[:, :], pool[di][:, 128 * ei:128 * (ei + 1)],
                               aon[di][:, s_], start=(di == 0), stop=(di == ND - 1))
            tm = C.sT.inc(h)
            C.v.wait_ge(C.sT.h, tm)
            C.sV.inc(C.v.tensor_scalar(ophi[:, s_], bank[:, :],
                                       bout[:, ei:ei + 1], None, ALU.add))
        C.v.wait_ge(C.sV.h, C.sV.n)
        vm = C.sV.inc(C.v.tensor_copy(
            opi[:, :].rearrange("p (r u) -> p r u", r=16),
            ophi[:, :].rearrange("p (r u) -> p r u", r=16)[:, :, ::-1]))
        C.v.wait_ge(C.sV.h, vm)
        xTr = xT[ei][:, :].rearrange("p (a b c) -> p a b c", a=16, b=8)
        vm = C.sV.inc(C.v.tensor_add(
            xTr, xTr,
            bass.AP(opi.tensor, opi[:, :].offset,
                    [list(opi[:, :].ap)[0], [8, 16], [1, 8], [128, 16]])))
    return (C.sV, C.sV.n)


# version 13 (final)
# speedup vs baseline: 1.1483x; 1.1483x over previous
import numpy as np
import contextlib
import concourse.bass as bass
import concourse.mybir as mybir
from concourse.bass_utils import run_bass_kernel_spmd

F32 = mybir.dt.float32
BF16 = mybir.dt.bfloat16
I32 = mybir.dt.int32
AF = mybir.ActivationFunctionType
ALU = mybir.AluOpType
AX = mybir.AxisListType

NCORE = 8
T = 2048
D = 1024
NT = T // 128
ND = D // 128


class Rec:
    def __init__(self, name):
        self.name = name
        self.q = []

    def op(self, meth, *a, **k):
        item = [meth, a, k, []]
        self.q.append(item)
        class H:
            def then_inc(self, sem, v=1):
                item[3].append((sem, v))
                return self
        return H()

    def __getattr__(self, meth):
        if meth.startswith("_") or meth in ("q", "name", "op", "replay"):
            raise AttributeError(meth)
        return lambda *a, **k: self.op(meth, *a, **k)

    def replay(self, eng):
        for meth, a, k, incs in self.q:
            instr = getattr(eng, meth)(*a, **k)
            for sem, v in incs:
                instr.then_inc(sem, v)


class Sem:
    def __init__(self, nc, stack, name):
        self.h = stack.enter_context(nc.semaphore(name))
        self.n = 0

    def inc(self, handle, k=1):
        handle.then_inc(self.h, k)
        self.n += k
        return self.n


class Ctx:
    def __init__(self):
        nc = bass.Bass("TRN2", target_bir_lowering=False)
        self.nc = nc
        self.stack = contextlib.ExitStack()
        self.sy = Rec("sync")
        self.g = Rec("gpsimd")
        self.v = Rec("vector")
        self.a = Rec("scalar")
        self.t = Rec("tensor")
        self.engs = [self.sy, self.g, self.v, self.a, self.t]
        self.dL = Sem(nc, self.stack, "dL")
        self.dG = Sem(nc, self.stack, "dG")
        self.dS = Sem(nc, self.stack, "dS")
        self.sV = Sem(nc, self.stack, "sV")
        self.sA = Sem(nc, self.stack, "sA")
        self.sT = Sem(nc, self.stack, "sT")
        self.sG = Sem(nc, self.stack, "sG")
        self.sC = Sem(nc, self.stack, "sC")
        self.ins = {}
        self.outs = {}
        self.sb = {}
        self.ps = {}
        self.dram = {}
        self.scoped = False
        self.static_names = set()
        self.gp = None
        self.gp_bump = 0

    def inp(self, name, shape, dtype=F32):
        if name not in self.ins:
            self.ins[name] = self.nc.dram_tensor(name, list(shape), dtype,
                                                 kind="ExternalInput").ap()
        return self.ins[name]

    def out(self, name, shape, dtype=F32):
        self.outs[name] = self.nc.dram_tensor(name, list(shape), dtype,
                                              kind="ExternalOutput").ap()
        return self.outs[name]

    def scratch(self, name, shape, dtype=F32):
        if name not in self.dram:
            self.dram[name] = self.nc.dram_tensor(name, list(shape), dtype,
                                                  kind="Internal").ap()
        return self.dram[name]

    GPSZ = 145664

    def sbuf(self, name, shape, dtype=F32, alias=None, off=0):
        if name in self.sb:
            return self.sb[name]
        if alias is not None:
            ar = self.sbuf(alias, [128, 16384], mybir.dt.uint8)
            nb = int(shape[1]) * mybir.dt.size(dtype)
            assert off + nb <= 16384, (name, off, nb)
            self.sb[name] = ar[:, off:off + nb].bitcast(dtype)[
                0:shape[0], 0:shape[1]]
        elif not self.scoped:
            self.sb[name] = self.stack.enter_context(
                self.nc.sbuf_tensor(name, list(shape), dtype))
            self.static_names.add(name)
        else:
            nb = int(shape[1]) * mybir.dt.size(dtype)
            a = (self.gp_bump + 31) // 32 * 32
            assert a + nb <= self.GPSZ, (name, a, nb, "GP arena overflow")
            self.gp_bump = a + nb
            gp = self.gp
            self.sb[name] = gp[:, a:a + nb].bitcast(dtype)[
                0:shape[0], 0:shape[1]]
        return self.sb[name]

    def scope_reset(self):
        for n in list(self.sb):
            if n not in self.static_names:
                del self.sb[n]
        self.gp_bump = 0

    def barrier(self):
        for eng in (self.sy, self.g, self.v, self.a, self.t):
            for s in (self.sV, self.sA, self.sT, self.sG, self.sC,
                      self.dL, self.dG, self.dS):
                if s.n:
                    eng.wait_ge(s.h, s.n)

    def free(self, name):
        if name in self.sb:
            h = self.sb.pop(name)
            try:
                h.free()
            except Exception:
                self.sb[name] = h

    def psum(self, name, shape, dtype=F32):
        if name not in self.ps:
            self.ps[name] = self.stack.enter_context(
                self.nc.psum_tensor(name, list(shape), dtype))
        return self.ps[name]

    def wait(self, eng, mark):
        if mark is not None:
            eng.wait_ge(mark[0].h, mark[1])

    def finish(self):
        nc = self.nc
        with self.stack:
            with nc.Block() as block:
                @block.sync
                def _(sync):
                    self.sy.replay(sync)

                @block.gpsimd
                def _(gpsimd):
                    self.g.replay(gpsimd)

                @block.vector
                def _(vector):
                    self.v.replay(vector)

                @block.scalar
                def _(scalar):
                    self.a.replay(scalar)

                @block.tensor
                def _(tensor):
                    self.t.replay(tensor)
        return nc



def _chunk128(vec):
    return np.ascontiguousarray(vec.reshape(-1, 128).T)


def stage_inputs(I):
    f = lambda a: np.ascontiguousarray(np.asarray(a, dtype=np.float32))
    x = f(I["L0_attn_a_x"]).reshape(16384, 1024)
    rel = f(I["L0_attn_a_rel_emb"])
    P0 = 4096
    Epad = np.empty((16, 16388), np.float32)
    for r in range(16):
        Epad[r, :P0] = rel[0, r]
        Epad[r, P0:P0 + 4096] = rel[:, r]
        Epad[r, P0 + 4096:] = rel[4095, r]
    EPLEN = 16000
    epads = []
    for cb in range(4):
        start = P0 + 4095 - 127 - 128 * cb - 7680
        assert start >= 0 and start + EPLEN <= Epad.shape[1]
        epads.append(np.ascontiguousarray(Epad[:, start:start + EPLEN]))

    tri = np.tril(np.ones((128, 128), np.float32), -1).T
    tri16 = np.zeros((16, 16), np.float32)
    for i in range(16):
        tri16[:i, i] = 1.0
    iot8 = np.broadcast_to(np.arange(8, dtype=np.float32), (128, 8)).copy()
    iden = np.eye(128, dtype=np.float32)
    m2T = np.tril(np.ones((128, 128), np.float32)).T * (1.0 / 512.0)
    m0T = np.tril(np.ones((128, 128), np.float32), -1) * (1.0 / 512.0)
    wq = f(I["L1_attn_a_wq"]).reshape(1024, 1024)
    wk = f(I["L1_attn_a_wk"]).reshape(1024, 1024)
    wv = f(I["L1_attn_a_wv"]).reshape(1024, 1024)
    wo = f(I["L1_attn_a_wo"]).reshape(1024, 1024)

    staged = []
    for c in range(NCORE):
        cb = c % 4
        d = {
            "xT": np.ascontiguousarray(x[2048 * c:2048 * (c + 1), :].T),
            "iden": iden, "tri": tri, "tri16": tri16, "iot8": iot8,
            "selrank": (np.arange(8) < c).astype(np.float32)[None, :],
            "selrank16": np.ascontiguousarray(np.broadcast_to(
                (np.arange(8) < c).astype(np.float32)[:, None], (8, 16))),
            "epad": epads[cb], "m2T": m2T, "m0T": m0T,
            "Winq": f(I["L0_attn_a_W_in"][:, 0:1024]),
            "Wink": f(I["L0_attn_a_W_in"][:, 1024:2048]),
            "Winv": f(I["L0_attn_a_W_in"][:, 2048:3072]),
            "binq_c": _chunk128(f(I["L0_attn_a_b_in"][0:1024])),
            "bink_c": _chunk128(f(I["L0_attn_a_b_in"][1024:2048])),
            "binv_r": f(I["L0_attn_a_b_in"][2048:3072]).reshape(1, 1024),
            "Wout": f(I["L0_attn_a_W_out"]),
            "bout_c": _chunk128(f(I["L0_attn_a_b_out"])),
            "w0": f(I["L0_ffn_weight"]),
            "b0c": _chunk128(f(I["L0_ffn_bias"])),
            "Wqkv": f(I["L0_attn_b_W_qkv"]),
            "bqkv_c": np.ascontiguousarray(
                f(I["L0_attn_b_b_qkv"]).reshape(-1, 128).T),
            "bqkv_vr": f(I["L0_attn_b_b_qkv"][2048:3072]).reshape(1, 1024),
            "haloidx": (np.arange(128)[:, None] + 128 * np.arange(8)[None, :]
                        + 1024 * max((c % 4) - 1, 0)).astype(np.int32),
            "hmask": np.full((128, 1), 1.0 if (c % 4) != 0 else 0.0, np.float32),
            "lnw_c": _chunk128(f(I["L0_attn_b_ln_w"])),
            "lnb_c": _chunk128(f(I["L0_attn_b_ln_b"])),
            "g1c": _chunk128(f(I["L1_attn_a_gamma1"])),
            "be1c": _chunk128(f(I["L1_attn_a_beta1"])),
            "wqf": wq, "wkf": wk, "wvf": wv, "wof": wo,
            "w1i": np.ascontiguousarray(
                f(I["L1_attn_a_w1"]).reshape(8, 128, 32, 128)
                .transpose(2, 1, 0, 3)),
            "w2i": np.ascontiguousarray(
                f(I["L1_attn_a_w2"]).reshape(32, 128, 8, 128)
                .transpose(2, 1, 0, 3)),
            "b1c": np.ascontiguousarray(
                f(I["L1_attn_a_b1"]).reshape(-1, 128).T),
            "b2c": _chunk128(f(I["L1_attn_a_b2"])),
            "Wg": f(I["L1_ffn_Wg"]),
            "We": f(I["L1_ffn_We"]),
            "bec": _chunk128(f(I["L1_ffn_be"])),
            "WqT7": f(I["L1_attn_b_Wq"].T),
            "WkT7": f(I["L1_attn_b_Wk"].T),
            "WvT7": f(I["L1_attn_b_Wv"].T),
            "bq7c": _chunk128(f(I["L1_attn_b_bq"])),
            "bk7c": _chunk128(f(I["L1_attn_b_bk"])),
            "bv7c": _chunk128(f(I["L1_attn_b_bv"])),
            "bv7r": f(I["L1_attn_b_bv"]).reshape(1, 1024),
        }
        staged.append(d)
    return staged



def prologue(C):
    xdram = C.inp("xT", [D, T])
    xT = [C.sbuf(f"xT{i}", [128, T]) for i in range(ND)]
    for i in range(ND):
        C.dL.inc(C.sy.dma_start(out=xT[i][:, :],
                                in_=xdram[128 * i:128 * (i + 1), :]), 16)
    idn = C.inp("iden", [128, 128])
    idf = C.sbuf("idf", [128, 128], F32)
    idb = C.sbuf("idb", [128, 128], BF16)
    C.dL.inc(C.sy.dma_start(out=idf[:, :], in_=idn), 16)
    C.dG.inc(C.g.dma_start(out=idb[:, :], in_=idn), 16)
    C.v.memset(C.sbuf("onesf", [1, 128], F32)[:, :], 1.0)
    C.v.memset(C.sbuf("onescf", [128, 1], F32)[:, :], 1.0)
    C.v.memset(C.sbuf("onescb", [128, 1], BF16)[:, :], 1.0)
    mk = C.sV.inc(C.v.memset(C.sbuf("onesb", [1, 128], BF16)[:, :], 1.0))
    for e in (C.v, C.a, C.t, C.g):
        e.wait_ge(C.dL.h, C.dL.n)
        e.wait_ge(C.dG.h, C.dG.n)
        e.wait_ge(C.sV.h, mk)
    return xT


def epilogue(C, xT, mark):
    ydram = C.out("yT", [D, T])
    C.wait(C.sy, mark)
    for i in range(ND):
        C.dS.inc(C.sy.dma_start(out=ydram[128 * i:128 * (i + 1), :],
                                in_=xT[i][:, :]), 16)
    C.sy.wait_ge(C.dS.h, C.dS.n)


def get_pb(C):
    big = C.psum("pbig", [128, 2048], F32)
    out = [big[:, 512 * i:512 * (i + 1)] for i in range(4)]
    out += [C.psum(f"pb{i}", [128, 512], F32) for i in range(4, 8)]
    out.append(big[:, :])
    return out


def emit_ln(C, xT, out_bf, mark, gamma=None, beta=None, eps=1e-5, apply=True, f32src=True):
    pb = get_pb(C)
    onesf = C.sb["onesf"]
    rows = C.sbuf("ln_rows", [1, 2 * T], F32, alias="scrA")
    sqc = [C.sbuf(f"ln_sqc{j}", [128, 512], BF16) for j in range(2)]
    tmpc = C.sbuf("ln_tmpc", [128, 512], F32)

    C.wait(C.v, mark)
    C.wait(C.t, mark)
    for qc in range(4):
        s_ = slice(512 * qc, 512 * (qc + 1))
        if qc > 0:
            C.t.wait_ge(C.sV.h, C.sV.n)
        onescol = C.sb["onescf"] if f32src else C.sb["onescb"]
        for i in range(ND):
            C.t.matmul(pb[4][0:1, :], onescol[:, :], xT[i][:, s_],
                       start=(i == 0), stop=(i == ND - 1))
        for i in range(ND):
            j = i % 2
            vm = C.sV.inc(C.v.tensor_mul(sqc[j][:, :], xT[i][:, s_], xT[i][:, s_]))
            C.t.wait_ge(C.sV.h, vm)
            tm = C.sT.inc(C.t.matmul(pb[5][0:1, :], C.sb["onescb"][:, :], sqc[j][:, :],
                                     start=(i == 0), stop=(i == ND - 1)))
            C.v.wait_ge(C.sT.h, tm)
        C.v.wait_ge(C.sT.h, C.sT.n)
        C.v.tensor_copy(rows[0:1, s_], pb[4][0:1, :])
        C.sV.inc(C.v.tensor_copy(rows[0:1, T + 512 * qc:T + 512 * (qc + 1)],
                                 pb[5][0:1, :]))
    C.v.wait_ge(C.sV.h, C.sV.n)
    vm = C.sV.inc(C.v.tensor_scalar(rows[0:1, 0:T], rows[0:1, 0:T],
                                    1.0 / 1024.0, None, ALU.mult))
    for qc in range(4):
        s_ = slice(512 * qc, 512 * (qc + 1))
        sT_ = slice(T + 512 * qc, T + 512 * (qc + 1))
        C.v.wait_ge(C.sV.h, C.sV.n)
        vm = C.sV.inc(C.v.tensor_mul(tmpc[0:1, :], rows[0:1, s_], rows[0:1, s_]))
        C.v.wait_ge(C.sV.h, vm)
        vm = C.sV.inc(C.v.tensor_scalar(rows[0:1, sT_], rows[0:1, sT_],
                                        1.0 / 1024.0, eps, ALU.mult, ALU.add))
        C.v.wait_ge(C.sV.h, vm)
        vm = C.sV.inc(C.v.tensor_sub(rows[0:1, sT_], rows[0:1, sT_], tmpc[0:1, :]))
    C.a.wait_ge(C.sV.h, C.sV.n)
    am = C.sA.inc(C.a.activation(rows[0:1, T:2 * T], rows[0:1, T:2 * T], AF.Ln))
    C.a.wait_ge(C.sA.h, am)
    am = C.sA.inc(C.a.activation(rows[0:1, T:2 * T], rows[0:1, T:2 * T],
                                 AF.Exp, scale=-0.5))
    C.v.wait_ge(C.sA.h, am)
    vm = C.sV.inc(C.v.scalar_tensor_tensor(rows[0:1, 0:T], rows[0:1, 0:T],
                                           -1.0, rows[0:1, T:2 * T],
                                           ALU.mult, ALU.mult))
    ab = rows
    if not apply:
        return (C.sV, vm)
    C.t.wait_ge(C.sV.h, vm)
    apm = {}
    for qc in range(4):
        s_ = slice(512 * qc, 512 * (qc + 1))
        ba, bb = (6, 7) if qc % 2 == 0 else (4, 5)
        C.wait(C.t, apm.get(ba))
        C.t.matmul(pb[ba][:, :], onesf[:, :],
                   ab[0:1, T + 512 * qc:T + 512 * (qc + 1)],
                   start=True, stop=True)
        tm = C.sT.inc(C.t.matmul(pb[bb][:, :], onesf[:, :], ab[0:1, s_],
                                 start=True, stop=True))
        C.v.wait_ge(C.sT.h, tm)
        for i in range(ND):
            vm = C.sV.inc(C.v.tensor_tensor(tmpc[:, :], xT[i][:, s_],
                                            pb[ba][:, :], ALU.mult))
            C.v.wait_ge(C.sV.h, vm)
            vm = C.sV.inc(C.v.tensor_tensor(out_bf[i][:, s_], tmpc[:, :],
                                            pb[bb][:, :], ALU.add))
        apm[ba] = apm[bb] = (C.sV, vm)
    lastmark = (C.sV, C.sV.n)
    if gamma is not None:
        gm = C.sb[gamma]
        bt = C.sb[beta]
        C.v.wait_ge(C.sV.h, C.sV.n)
        for i in range(ND):
            h = C.v.tensor_scalar(out_bf[i][:, :], out_bf[i][:, :],
                                  gm[:, i:i + 1], bt[:, i:i + 1],
                                  ALU.mult, ALU.add)
        lastmark = (C.sV, C.sV.inc(h))
    return lastmark


def emit_transpose_f2t(C, srcT, dst_tok, tile_idx, mark, dtype_note="f32"):
    pb = get_pb(C)
    idf = C.sb["idf"]
    s_ = slice(128 * tile_idx, 128 * (tile_idx + 1))
    C.wait(C.t, mark)
    for i in range(ND):
        if i > 0:
            C.t.wait_ge(C.sV.h, C.sV.n)
        tm = C.sT.inc(C.t.transpose(pb[3][:, 0:128], srcT[i][:, s_], idf[:, :]))
        C.v.wait_ge(C.sT.h, tm)
        C.sV.inc(C.v.tensor_copy(dst_tok[:, 128 * i:128 * (i + 1)],
                                 pb[3][:, 0:128]))
    return (C.sV, C.sV.n)


def emit_quant_tile(C, y, q_out, mark, scr):
    v = C.v
    sm = scr["sm"]
    sr = scr["sr"]
    t0 = scr["t0"]
    t1 = scr["t1"]
    t2 = scr["t2"]
    ti = scr["ti"]
    C.wait(v, mark)

    def step(h):
        return C.sV.inc(h)

    def chain(h):
        n = C.sV.inc(h)
        v.wait_ge(C.sV.h, n)
        return n

    g32 = y[:, :].rearrange("p (g k) -> p g k", k=32)
    chain(v.tensor_reduce(sm[:, :], g32, AX.X, ALU.max, apply_absolute_value=True))
    chain(v.tensor_scalar(sm[:, :], sm[:, :], float(np.float32(1.0 / 7.5)),
                          1e-30, ALU.mult, ALU.max))
    chain(v.reciprocal(sr[:, :], sm[:, :]))
    _s = sr[:, :]
    rsb = bass.AP(_s.tensor, _s.offset, [list(_s.ap)[0], [1, 32], [0, 32]])
    chain(v.tensor_tensor(t0[:, :].rearrange("p (g k) -> p g k", k=32),
                          g32, rsb, ALU.mult))
    stp = scr["stp"]
    chain(v.tensor_mul(t1[:, :], t0[:, :], t0[:, :]))
    chain(v.tensor_scalar(t2[:, :], t1[:, :], 4.0, None, ALU.is_gt))
    chain(v.tensor_scalar(t1[:, :], t1[:, :], 16.0, None, ALU.is_gt))
    chain(v.tensor_scalar(stp[:, :], t2[:, :], 0.125, 0.125, ALU.mult, ALU.add))
    chain(v.scalar_tensor_tensor(stp[:, :], t1[:, :], 0.25, stp[:, :],
                                 ALU.mult, ALU.add))
    chain(v.tensor_scalar(t2[:, :], t2[:, :], -4.0, 8.0, ALU.mult, ALU.add))
    chain(v.scalar_tensor_tensor(t2[:, :], t1[:, :], -2.0, t2[:, :],
                                 ALU.mult, ALU.add))
    chain(v.tensor_tensor(t2[:, :], t0[:, :], t2[:, :], ALU.mult))
    chain(v.tensor_scalar(t2[:, :], t2[:, :], -1.0, 0.5, ALU.mult, ALU.add))
    chain(v.tensor_copy(ti[:, :], t2[:, :]))
    chain(v.tensor_copy(t0[:, :], ti[:, :]))
    chain(v.tensor_tensor(t1[:, :], t0[:, :], t2[:, :], ALU.is_gt))
    chain(v.tensor_tensor(t0[:, :], t0[:, :], t1[:, :], ALU.subtract))
    chain(v.tensor_scalar(t0[:, :], t0[:, :], -1.0, None, ALU.mult))
    chain(v.tensor_tensor(t0[:, :], t0[:, :], stp[:, :], ALU.mult))
    _m = sm[:, :]
    smb = bass.AP(_m.tensor, _m.offset, [list(_m.ap)[0], [1, 32], [0, 32]])
    chain(v.tensor_tensor(q_out[:, :].rearrange("p (g k) -> p g k", k=32),
                          t0[:, :].rearrange("p (g k) -> p g k", k=32),
                          smb, ALU.mult))
    return (C.sV, C.sV.n)


def emit_s2(C, xT, mark, tag):
    pb = get_pb(C)
    idb = C.sb["idb"]
    lnout = None
    mark = emit_ln(C, xT, lnout, mark, apply=False)
    ab = C.sb["ln_rows"]
    abt = C.sbuf("s2_abt", [128, 2 * NT], F32)
    abd = C.scratch("s2_abd", [1, 2 * T])
    C.wait(C.sy, mark)
    C.dL.inc(C.sy.dma_start(out=abd, in_=ab[:, :]), 16)
    C.sy.wait_ge(C.dL.h, C.dL.n)
    C.dL.inc(C.sy.dma_start(
        out=abt[:, 0:NT],
        in_=bass.AP(abd.tensor, T, [[1, 128], [128, NT]]),
        allow_slow_non_contiguous=True), 16)
    C.dL.inc(C.sy.dma_start(
        out=abt[:, NT:2 * NT],
        in_=bass.AP(abd.tensor, 0, [[1, 128], [128, NT]]),
        allow_slow_non_contiguous=True), 16)
    abmark = (C.dL, C.dL.n)

    wdT = [C.sbuf(f"s2_wdT{i}", [128, D], BF16) for i in range(ND)]
    d_wdT = C.scratch("s2_dwdT", [D, D], BF16)
    w0 = C.inp("w0", [D, D])
    wf = C.sbuf("s2_wf", [128, D], F32)
    wq_ = C.sbuf("s2_wq", [128, D], BF16)
    scr = {
        "sm": C.sbuf("q_sm", [128, 32], F32),
        "sr": C.sbuf("q_sr", [128, 32], F32),
        "t0": C.sbuf("q_t0", [128, D], F32),
        "t1": C.sbuf("q_t1", [128, D], F32),
        "t2": C.sbuf("q_t2", [128, D], F32),
        "ti": C.sbuf("q_ti", [128, D], I32),
        "stp": C.sbuf("q_stp", [128, D], F32),
    }
    wm = None
    tpm = {}
    if tag == "s4":
        for ki in range(ND):
            C.dL.inc(C.sy.dma_start(out=wdT[ki][:, :],
                                    in_=d_wdT[128 * ki:128 * (ki + 1), :]), 16)
        wlm = C.dL.n
        for e in (C.v, C.t):
            e.wait_ge(C.dL.h, wlm)
        wm = (C.dL, wlm)
    else:
      for oi in range(ND):
        C.wait(C.sy, wm)
        ld = C.dL.inc(C.sy.dma_start(out=wf[:, :],
                                     in_=w0[128 * oi:128 * (oi + 1), :]), 16)
        wm = emit_quant_tile(C, wf, wq_, (C.dL, ld), scr)
        C.wait(C.t, wm)
        for ki in range(ND):
            bx = 2 + (ki % 2)
            C.wait(C.t, tpm.get(bx))
            tm = C.sT.inc(C.t.transpose(
                pb[bx][:, 0:256].bitcast(BF16)[:, 0:128],
                wq_[:, 128 * ki:128 * (ki + 1)], idb[:, :]))
            C.v.wait_ge(C.sT.h, tm)
            vm = C.sV.inc(C.v.tensor_copy(wdT[ki][:, 128 * oi:128 * (oi + 1)],
                                          pb[bx][:, 0:256].bitcast(BF16)[:, 0:128]))
            tpm[bx] = (C.sV, vm)
        wm = (C.sV, C.sV.n)
    if tag != "s4":
        C.wait(C.sy, wm)
        for ki in range(ND):
            C.dL.inc(C.sy.dma_start(out=d_wdT[128 * ki:128 * (ki + 1), :],
                                    in_=wdT[ki][:, :]), 16)

    xdT = [C.sbuf(f"s2_xdT{i}", [128, T], BF16) for i in range(ND)]
    xtok = C.sbuf("s2_xtok", [128, D], F32)
    xqt = C.sbuf("s2_xq", [128, D], BF16)
    qm = wm
    for tt in range(NT):
        qm = emit_transpose_f2t(C, xT, xtok, tt, qm)
        C.wait(C.v, qm)
        C.wait(C.v, abmark)
        vm = C.sV.inc(C.v.tensor_scalar(
            xtok[:, :], xtok[:, :], abt[:, tt:tt + 1],
            abt[:, NT + tt:NT + tt + 1], ALU.mult, ALU.add))
        qm = emit_quant_tile(C, xtok, xqt, (C.sV, vm), scr)
        C.wait(C.t, qm)
        for ki in range(ND):
            bx = 2 + (ki % 2)
            C.wait(C.t, tpm.get(bx))
            tm = C.sT.inc(C.t.transpose(
                pb[bx][:, 0:256].bitcast(BF16)[:, 0:128],
                xqt[:, 128 * ki:128 * (ki + 1)], idb[:, :]))
            C.v.wait_ge(C.sT.h, tm)
            vm = C.sV.inc(C.v.tensor_copy(xdT[ki][:, 128 * tt:128 * (tt + 1)],
                                          pb[bx][:, 0:256].bitcast(BF16)[:, 0:128]))
            tpm[bx] = (C.sV, vm)
        qm = (C.sV, C.sV.n)

    b0c = C.sbuf("s2_b0c", [128, ND], F32)
    C.dL.inc(C.sy.dma_start(out=b0c[:, :], in_=C.inp("b0c", [128, ND])), 16)
    bm = (C.dL, C.dL.n)
    C.wait(C.t, qm)
    C.wait(C.v, bm)
    mmk = {}
    for oi in range(ND):
        for qc in range(4):
            bx = qc % 2
            bank = pb[bx]
            s_ = slice(512 * qc, 512 * (qc + 1))
            C.wait(C.t, mmk.get(bx))
            for ki in range(ND):
                h = C.t.matmul(bank[:, :], wdT[ki][:, 128 * oi:128 * (oi + 1)],
                               xdT[ki][:, s_], start=(ki == 0), stop=(ki == ND - 1))
            tm = C.sT.inc(h)
            C.v.wait_ge(C.sT.h, tm)
            vm = C.sV.inc(C.v.scalar_tensor_tensor(
                xT[oi][:, s_], bank[:, :], b0c[:, oi:oi + 1], xT[oi][:, s_],
                ALU.add, ALU.add))
            mmk[bx] = (C.sV, vm)
    return (C.sV, C.sV.n)



EMITTERS = {}


def build_program(subset):
    C = Ctx()
    xT = prologue(C)
    get_pb(C)
    C.gp = C.stack.enter_context(
        C.nc.sbuf_tensor("GP", [128, Ctx.GPSZ], mybir.dt.uint8))
    C.static_names.add("GP")
    C.scoped = True
    mark = None
    for k in subset:
        C.scope_reset()
        C.barrier()
        mark = EMITTERS[k](C, xT, mark)
    C.barrier()
    epilogue(C, xT, mark)
    return C


_PROG_CACHE = {}


def _get_prog(subset):
    key = tuple(subset)
    if key not in _PROG_CACHE:
        C = build_program(subset)
        nc = C.finish()
        _PROG_CACHE[key] = (nc, sorted(C.ins.keys()))
    return _PROG_CACHE[key]


def run_subset(subset, staged):
    nc, in_names = _get_prog(subset)
    in_maps = [{k: staged[c][k] for k in in_names} for c in range(NCORE)]
    res = run_bass_kernel_spmd(nc, in_maps, core_ids=list(range(NCORE)))
    for c in range(NCORE):
        staged[c]["xT"] = np.ascontiguousarray(res.results[c]["yT"])
    return staged


LAUNCH_PLAN = [[1, 2, 3, 4, 5, 6, 7, 8]]


def kernel(**inputs):
    staged = stage_inputs(inputs)
    for subset in LAUNCH_PLAN:
        staged = run_subset(subset, staged)
    out = np.empty((16384, 1024), np.float32)
    for c in range(NCORE):
        out[2048 * c:2048 * (c + 1), :] = staged[c]["xT"].T
    return out.reshape(2, 8192, 1024)


def _register():
    EMITTERS[2] = lambda C, xT, mark: emit_s2(C, xT, mark, "s2")
    EMITTERS[7] = lambda C, xT, mark: emit_s7(C, xT, mark)
    EMITTERS[6] = lambda C, xT, mark: emit_s6(C, xT, mark, "s6")
    EMITTERS[3] = lambda C, xT, mark: emit_s3(C, xT, mark)
    EMITTERS[8] = lambda C, xT, mark: emit_s6(C, xT, mark, "s8")
    EMITTERS[4] = lambda C, xT, mark: emit_s2(C, xT, mark, "s4")


_register()


def emit_s7(C, xT, mark, tag="s7"):
    pb = get_pb(C)
    onesb = C.sb["onesb"]
    xn = [C.sbuf(f"xn{i}", [128, T], BF16) for i in range(ND)]
    mark = emit_ln(C, xT, xn, mark)

    w7 = {}
    for nm in ("WqT7", "WkT7", "WvT7"):
        dr = C.inp(nm, [D, D])
        w7[nm] = [C.sbuf(f"{nm}_{i}", [128, D], BF16) for i in range(ND)]
        for i in range(ND):
            C.dG.inc(C.g.dma_start(out=w7[nm][i][:, :],
                                   in_=dr[128 * i:128 * (i + 1), :]), 16)
    bq = C.sbuf("s7_bq", [128, ND], F32)
    bk = C.sbuf("s7_bk", [128, ND], F32)
    bvr = C.sbuf("s7_bvr", [1, D], BF16)
    C.dL.inc(C.sy.dma_start(out=bq[:, :], in_=C.inp("bq7c", [128, ND])), 16)
    C.dL.inc(C.sy.dma_start(out=bk[:, :], in_=C.inp("bk7c", [128, ND])), 16)
    C.dG.inc(C.g.dma_start(out=bvr[:, :], in_=C.inp("bv7r", [1, D])), 16)
    for e in (C.v, C.a, C.t):
        e.wait_ge(C.dL.h, C.dL.n)
        e.wait_ge(C.dG.h, C.dG.n)

    qT = [C.sbuf(f"s7_qT{i}", [128, 512], BF16) for i in range(ND)]
    kT = [C.sbuf(f"s7_kT{i}", [128, 512], BF16) for i in range(ND)]
    vtk = [C.sbuf(f"s7_v{i}", [128, D], BF16) for i in range(4)]
    attn = C.sbuf("s7_attn", [128, 512], BF16)
    attnT = C.sbuf("s7_attnT", [128, 512], BF16)
    mx = C.sbuf("s7_mx", [128, 1], F32)
    nm_ = C.sbuf("s7_nm", [128, 1], F32)
    asum = C.sbuf("s7_as", [128, 1], F32)
    rs = C.sbuf("s7_rs", [128, 1], F32)
    idb = C.sb["idb"]

    C.wait(C.t, mark)
    s7m = {}
    for cn in range(4):
        s7 = slice(512 * cn, 512 * (cn + 1))
        for ei in range(ND):
            for (wnm, dstl, bc) in (("WqT7", qT, bq), ("WkT7", kT, bk)):
                bx = 0 if wnm == "WqT7" else 1
                bank = pb[bx]
                C.wait(C.t, s7m.get(bx))
                for di in range(ND):
                    h = C.t.matmul(bank[:, :],
                                   w7[wnm][di][:, 128 * ei:128 * (ei + 1)],
                                   xn[di][:, s7], start=(di == 0), stop=(di == ND - 1))
                tm = C.sT.inc(h)
                C.v.wait_ge(C.sT.h, tm)
                vm = C.sV.inc(C.v.tensor_scalar(dstl[ei][:, :], bank[:, :],
                                                bc[:, ei:ei + 1], None, ALU.add))
                s7m[bx] = (C.sV, vm)
        for ts_ in range(4):
            tsl = slice(512 * cn + 128 * ts_, 512 * cn + 128 * (ts_ + 1))
            for eh in range(2):
                esl = slice(512 * eh, 512 * (eh + 1))
                bx = (2, 7)[(2 * ts_ + eh) % 2]
                bank = pb[bx]
                C.wait(C.t, s7m.get(bx))
                for di in range(ND):
                    C.t.matmul(bank[:, :], xn[di][:, tsl],
                               w7["WvT7"][di][:, esl], start=(di == 0), stop=False)
                h = C.t.matmul(bank[:, :], onesb[:, :], bvr[:, esl],
                               start=False, stop=True)
                tm = C.sT.inc(h)
                C.v.wait_ge(C.sT.h, tm)
                vm = C.sV.inc(C.v.tensor_copy(vtk[ts_][:, esl], bank[:, :]))
                s7m[bx] = (C.sV, vm)
        vmark = C.sV.n
        for qs in range(4):
            qsl = slice(128 * qs, 128 * (qs + 1))
            C.wait(C.t, s7m.get(0))
            C.wait(C.t, s7m.get(1))
            C.wait(C.t, s7m.get(3))
            for ei in range(ND):
                h = C.t.matmul(pb[3][:, :], qT[ei][:, qsl], kT[ei][:, :],
                               start=(ei == 0), stop=(ei == ND - 1))
            tm = C.sT.inc(h)
            C.v.wait_ge(C.sT.h, tm)
            vm = C.sV.inc(C.v.reduce_max(mx[:, :], pb[3][:, :], AX.X))
            C.v.wait_ge(C.sV.h, vm)
            vm = C.sV.inc(C.v.tensor_scalar(nm_[:, :], mx[:, :], -1.0, None,
                                            ALU.mult))
            C.a.wait_ge(C.sV.h, vm)
            am = C.sA.inc(C.a.activation(attn[:, :], pb[3][:, :], AF.Exp,
                                         bias=nm_[:, :], accum_out=asum[:, :]))
            s7m[3] = (C.sA, am)
            C.v.wait_ge(C.sA.h, am)
            vm = C.sV.inc(C.v.reciprocal(rs[:, :], asum[:, :]))
            C.v.wait_ge(C.sV.h, vm)
            vm = C.sV.inc(C.v.tensor_scalar(attn[:, :], attn[:, :], rs[:, :],
                                            None, ALU.mult))
            C.t.wait_ge(C.sV.h, vm)
            for kk in range(4):
                if kk > 0:
                    C.t.wait_ge(C.sV.h, C.sV.n)
                tm = C.sT.inc(C.t.transpose(
                    pb[4][:, 0:256].bitcast(BF16)[:, 0:128],
                    attn[:, 128 * kk:128 * (kk + 1)], idb[:, :]))
                C.v.wait_ge(C.sT.h, tm)
                C.sV.inc(C.v.tensor_copy(attnT[:, 128 * kk:128 * (kk + 1)],
                                         pb[4][:, 0:256].bitcast(BF16)[:, 0:128]))
            C.t.wait_ge(C.sV.h, C.sV.n)
            for ei in range(ND):
                bank = pb[5] if ei % 2 == 0 else pb[6]
                if ei >= 2:
                    C.t.wait_ge(C.sV.h, C.sV.n)
                for kk in range(4):
                    h = C.t.matmul(bank[:, 0:128],
                                   vtk[kk][:, 128 * ei:128 * (ei + 1)],
                                   attnT[:, 128 * kk:128 * (kk + 1)],
                                   start=(kk == 0), stop=(kk == 3))
                tm = C.sT.inc(h)
                C.v.wait_ge(C.sT.h, tm)
                C.sV.inc(C.v.tensor_tensor(
                    xT[ei][:, 512 * cn + 128 * qs:512 * cn + 128 * (qs + 1)],
                    xT[ei][:, 512 * cn + 128 * qs:512 * cn + 128 * (qs + 1)],
                    bank[:, 0:128], ALU.add))
    return (C.sV, C.sV.n)


def emit_s6(C, xT, mark, tag="s6"):
    pb = get_pb(C)
    onesf = C.sb["onesf"]
    idf = C.sb["idf"]
    xn = [C.sbuf(f"xn{i}", [128, T], BF16) for i in range(ND)]
    mark = emit_ln(C, xT, xn, mark)

    Wg = C.inp("Wg", [D, 8])
    wg = [C.sbuf(f"s6_wg{i}", [128, 8], BF16) for i in range(ND)]
    for i in range(ND):
        C.dG.inc(C.g.dma_start(out=wg[i][:, :], in_=Wg[128 * i:128 * (i + 1), :]), 16)
    We = C.inp("We", [D, D])
    we = [C.sbuf(f"s6_we{i}", [128, D], BF16) for i in range(ND)]
    for i in range(ND):
        C.dG.inc(C.g.dma_start(out=we[i][:, :], in_=We[128 * i:128 * (i + 1), :]), 16)
    trib = C.sbuf("s6_trib", [128, 128], BF16)
    C.dG.inc(C.g.dma_start(out=trib[:, :], in_=C.inp("tri", [128, 128])), 16)
    tri16 = C.sbuf("s6_tri16", [16, 16], F32)
    C.dL.inc(C.sy.dma_start(out=tri16[:, :], in_=C.inp("tri16", [16, 16])), 16)
    iot8 = C.sbuf("s6_iot8", [128, 8], F32)
    C.dL.inc(C.sy.dma_start(out=iot8[:, :], in_=C.inp("iot8", [128, 8])), 16)
    sel16 = C.sbuf("s6_sel16", [8, 16], F32)
    C.dL.inc(C.sy.dma_start(out=sel16[:, :], in_=C.inp("selrank16", [8, 16])), 16)
    bec = C.sbuf("s6_bec", [128, ND], F32)
    C.dL.inc(C.sy.dma_start(out=bec[:, :], in_=C.inp("bec", [128, ND])), 16)
    for e in (C.v, C.a, C.t, C.g):
        e.wait_ge(C.dL.h, C.dL.n)
        e.wait_ge(C.dG.h, C.dG.n)

    lgT = C.sbuf("s6_lgT", [8, T], F32)
    lgt = C.sbuf("s6_lgt", [128, 8], F32)
    Mb = [C.sbuf(f"s6_M{t}", [128, 8], BF16) for t in range(NT)]
    gate = C.sbuf("s6_gate", [128, NT], F32)
    estar = C.sbuf("s6_est", [128, 1], F32)
    mx = C.sbuf("s6_mx", [128, 1], F32)
    tmp8 = C.sbuf("s6_t8", [128, 8], F32)
    cntrow = C.sbuf("s6_cntr", [1, 128], F32)
    cnt16 = C.sbuf("s6_c16", [16, 8], F32)
    gath = C.sbuf("s6_gath", [8, 8], F32)
    offsetG = C.sbuf("s6_offG", [1, 8], F32)
    prefall = C.sbuf("s6_pref", [16, 8], F32)
    coefc = C.sbuf("s6_coefc", [128, NT], F32)
    coefrow = C.sbuf("s6_crow", [1, T], F32)
    tmpe = C.sbuf("s6_tmpe", [128, 512], F32)
    d_cc_in = C.scratch(f"{tag}_ccin", [1, 8])
    d_cc_out = C.scratch(f"{tag}_ccout", [8, 8])
    d_cnt = C.scratch(f"{tag}_cnt", [1, 128])
    d_coef = C.scratch(f"{tag}_coef", [128, NT])

    C.wait(C.t, mark)
    for qc in range(4):
        s_ = slice(512 * qc, 512 * (qc + 1))
        if qc:
            C.t.wait_ge(C.sV.h, C.sV.n)
        for di in range(ND):
            h = C.t.matmul(pb[0][0:8, :], wg[di][:, :], xn[di][:, s_],
                           start=(di == 0), stop=(di == ND - 1))
        tm = C.sT.inc(h)
        C.v.wait_ge(C.sT.h, tm)
        C.sV.inc(C.v.tensor_copy(lgT[:, s_], pb[0][0:8, :]))
    for t in range(NT):
        tsl = slice(128 * t, 128 * (t + 1))
        C.t.wait_ge(C.sV.h, C.sV.n)
        tm = C.sT.inc(C.t.transpose(pb[1][0:128, 0:8], lgT[0:8, tsl],
                                    idf[0:8, 0:8]))
        C.v.wait_ge(C.sT.h, tm)
        vm = C.sV.inc(C.v.tensor_copy(lgt[:, :], pb[1][0:128, 0:8]))
        C.v.wait_ge(C.sV.h, vm)
        vm = C.sV.inc(C.v.reduce_max(mx[:, :], lgt[:, :], AX.X))
        C.v.wait_ge(C.sV.h, vm)
        vm = C.sV.inc(C.v.tensor_scalar(tmp8[:, :], lgt[:, :], mx[:, :], None,
                                        ALU.subtract))
        C.a.wait_ge(C.sV.h, vm)
        am = C.sA.inc(C.a.activation(tmp8[:, :], tmp8[:, :], AF.Exp,
                                     accum_out=estar[:, :]))
        C.v.wait_ge(C.sA.h, am)
        vm = C.sV.inc(C.v.reciprocal(gate[:, t:t + 1], estar[:, :]))
        C.v.wait_ge(C.sV.h, vm)
        vm = C.sV.inc(C.v.tensor_scalar(tmp8[:, :], lgt[:, :], mx[:, :], None,
                                        ALU.is_equal))
        C.v.wait_ge(C.sV.h, vm)
        C.sV.inc(C.v.tensor_copy(Mb[t][:, :], tmp8[:, :]))
        vm = C.sV.inc(C.v.tensor_scalar(tmp8[:, :], tmp8[:, :], -999.0, 999.0,
                                        ALU.mult, ALU.add))
        C.v.wait_ge(C.sV.h, vm)
        vm = C.sV.inc(C.v.tensor_add(tmp8[:, :], tmp8[:, :], iot8[:, :]))
        C.v.wait_ge(C.sV.h, vm)
        vm = C.sV.inc(C.v.tensor_reduce(estar[:, :], tmp8[:, :], AX.X, ALU.min))
        C.v.wait_ge(C.sV.h, vm)
        vm = C.sV.inc(C.v.tensor_scalar(Mb[t][:, :], iot8[:, :], estar[:, :],
                                        None, ALU.is_equal))
        C.t.wait_ge(C.sV.h, vm)
        h = C.t.matmul(pb[2][0:1, 8 * t:8 * (t + 1)], C.sb["onescb"][:, :],
                       Mb[t][:, :], start=(t == 0), stop=(t == NT - 1))
    tm = C.sT.inc(h)
    C.v.wait_ge(C.sT.h, tm)
    vm = C.sV.inc(C.v.tensor_copy(cntrow[:, :], pb[2][0:1, 0:128]))
    C.v.wait_ge(C.sV.h, vm)
    _c = cntrow[0:1, :]
    c3 = bass.AP(_c.tensor, _c.offset, [list(_c.ap)[0], [1, 8], [8, 16]])
    vm = C.sV.inc(C.v.tensor_reduce(offsetG[:, :], c3, AX.X, ALU.add))
    C.wait(C.sy, (C.sV, vm))
    C.dL.inc(C.sy.dma_start(out=d_cc_in, in_=offsetG[:, :]), 16)
    C.g.wait_ge(C.dL.h, C.dL.n)
    cm = C.sC.inc(C.g.collective_compute(
        "AllGather", ALU.bypass, replica_groups=[list(range(8))],
        ins=[d_cc_in.opt()], outs=[d_cc_out.opt()]))
    C.sy.wait_ge(C.sC.h, cm)
    C.dL.inc(C.sy.dma_start(out=gath[:, :], in_=d_cc_out), 16)
    C.dL.inc(C.sy.dma_start(out=d_cnt, in_=cntrow[:, :]), 16)
    C.sy.wait_ge(C.dL.h, C.dL.n)
    C.dL.inc(C.sy.dma_start(
        out=cnt16[:, :], in_=bass.AP(d_cnt.tensor, 0, [[8, 16], [1, 8]])), 16)
    ldm = C.dL.n
    C.t.wait_ge(C.dL.h, ldm)
    C.t.matmul(pb[3][0:16, 0:8], sel16[:, :], gath[:, :], start=True, stop=False)
    tm = C.sT.inc(C.t.matmul(pb[3][0:16, 0:8], tri16[:, :], cnt16[:, :],
                             start=False, stop=True))
    C.v.wait_ge(C.sT.h, tm)
    vm = C.sV.inc(C.v.tensor_copy(prefall[:, :], pb[3][0:16, 0:8]))
    d_pref = C.scratch(f"{tag}_prefd", [16, 8])
    prefrow = C.sbuf("s6_prow", [1, 128], F32)
    C.wait(C.sy, (C.sV, vm))
    C.dL.inc(C.sy.dma_start(out=d_pref, in_=prefall[:, :]), 16)
    C.sy.wait_ge(C.dL.h, C.dL.n)
    C.dL.inc(C.sy.dma_start(out=prefrow[:, :],
                            in_=bass.AP(d_pref.tensor, 0, [[0, 1], [1, 128]])), 16)
    pfm = C.dL.n
    C.t.wait_ge(C.dL.h, pfm)
    for t in range(NT):
        C.t.wait_ge(C.sV.h, C.sV.n)
        C.t.matmul(pb[4][:, 0:8], trib[:, :], Mb[t][:, :], start=True, stop=False)
        h = C.t.matmul(pb[4][:, 0:8], onesf[:, :], prefrow[0:1, 8 * t:8 * (t + 1)],
                       start=False, stop=True)
        tm = C.sT.inc(h)
        C.v.wait_ge(C.sT.h, tm)
        vm = C.sV.inc(C.v.tensor_tensor(tmp8[:, :], pb[4][:, 0:8], Mb[t][:, :],
                                        ALU.mult))
        C.v.wait_ge(C.sV.h, vm)
        vm = C.sV.inc(C.v.tensor_reduce(estar[:, :], tmp8[:, :], AX.X, ALU.add))
        C.v.wait_ge(C.sV.h, vm)
        vm = C.sV.inc(C.v.tensor_scalar(estar[:, :], estar[:, :], 2048.0, None,
                                        ALU.is_lt))
        C.v.wait_ge(C.sV.h, vm)
        C.sV.inc(C.v.tensor_mul(coefc[:, t:t + 1], estar[:, :], gate[:, t:t + 1]))
    C.wait(C.sy, (C.sV, C.sV.n))
    C.dL.inc(C.sy.dma_start(out=d_coef, in_=coefc[:, :]), 16)
    C.sy.wait_ge(C.dL.h, C.dL.n)
    C.dL.inc(C.sy.dma_start(
        out=coefrow[:, :],
        in_=bass.AP(d_coef.tensor, 0, [[0, 1], [1, NT], [NT, 128]]),
        allow_slow_non_contiguous=True), 16)
    ldm2 = C.dL.n
    C.t.wait_ge(C.dL.h, ldm2)
    for qc in range(4):
        s_ = slice(512 * qc, 512 * (qc + 1))
        C.t.wait_ge(C.sV.h, C.sV.n)
        C.t.matmul(pb[5][:, :], onesf[:, :], coefrow[0:1, s_], start=True,
                   stop=True)
        for ei in range(ND):
            bank = pb[6] if ei % 2 == 0 else pb[7]
            if ei >= 2:
                C.t.wait_ge(C.sV.h, C.sV.n)
            for di in range(ND):
                h = C.t.matmul(bank[:, :], we[di][:, 128 * ei:128 * (ei + 1)],
                               xn[di][:, s_], start=(di == 0), stop=(di == ND - 1))
            tm = C.sT.inc(h)
            C.v.wait_ge(C.sT.h, tm)
            vm = C.sV.inc(C.v.tensor_scalar(tmpe[:, :], bank[:, :],
                                            bec[:, ei:ei + 1], None, ALU.add))
            C.v.wait_ge(C.sV.h, vm)
            vm = C.sV.inc(C.v.tensor_tensor(tmpe[:, :], tmpe[:, :], pb[5][:, :],
                                            ALU.mult))
            C.v.wait_ge(C.sV.h, vm)
            C.sV.inc(C.v.tensor_add(xT[ei][:, s_], xT[ei][:, s_], tmpe[:, :]))
    return (C.sV, C.sV.n)


def emit_s3(C, xT, mark, tag="s3"):
    pb = get_pb(C)
    onesb = C.sb["onesb"]
    onesf = C.sb["onesf"]
    xn = [C.sbuf(f"xn{i}", [128, T], BF16) for i in range(ND)]
    mark = emit_ln(C, xT, xn, mark)

    wq = C.inp("Wqkv", [D, 3 * D])
    wqkv = [C.sbuf(f"s3_w{i}", [128, 3 * D], BF16) for i in range(ND)]
    for i in range(ND):
        C.dG.inc(C.g.dma_start(out=wqkv[i][:, :],
                               in_=wq[128 * i:128 * (i + 1), :]), 16)
    bqkv = C.sbuf("s3_b", [128, 24], F32)
    C.dL.inc(C.sy.dma_start(out=bqkv[:, :], in_=C.inp("bqkv_c", [128, 24])), 16)
    bvr = C.sbuf("s3_bvr", [1, D], BF16)
    C.dG.inc(C.g.dma_start(out=bvr[:, :], in_=C.inp("bqkv_vr", [1, D])), 16)
    m0T = C.sbuf("s3_m0T", [128, 128], F32)
    m2T = C.sbuf("s3_m2T", [128, 128], F32)
    C.dL.inc(C.sy.dma_start(out=m0T[:, :], in_=C.inp("m0T", [128, 128])), 16)
    C.dL.inc(C.sy.dma_start(out=m2T[:, :], in_=C.inp("m2T", [128, 128])), 16)
    lnw = C.sbuf("s3_lnw", [128, ND], F32)
    lnb = C.sbuf("s3_lnb", [128, ND], F32)
    C.dL.inc(C.sy.dma_start(out=lnw[:, :], in_=C.inp("lnw_c", [128, ND])), 16)
    C.dL.inc(C.sy.dma_start(out=lnb[:, :], in_=C.inp("lnb_c", [128, ND])), 16)
    hidx = C.inp("haloidx", [128, ND], I32)
    for e in (C.v, C.a, C.t, C.g):
        e.wait_ge(C.dL.h, C.dL.n)
        e.wait_ge(C.dG.h, C.dG.n)

    d_tail = C.scratch(f"{tag}_tail", [D, 256], BF16)
    d_gath = C.scratch(f"{tag}_gath", [4 * D, 256], BF16)
    halo = [C.sbuf(f"s3_halo{i}", [128, 256], BF16) for i in range(ND)]
    C.wait(C.sy, mark)
    for i in range(ND):
        C.dL.inc(C.sy.dma_start(out=d_tail[128 * i:128 * (i + 1), :],
                                in_=xn[i][:, T - 256:T]), 16)
    C.g.wait_ge(C.dL.h, C.dL.n)
    cm = C.sC.inc(C.g.collective_compute(
        "AllGather", ALU.bypass, replica_groups=[[0, 1, 2, 3], [4, 5, 6, 7]],
        ins=[d_tail.opt()], outs=[d_gath.opt()]))
    for i in range(ND):
        C.v.memset(halo[i][:, :], 0.0)
    C.sV.inc(C.v.tensor_copy(halo[0][:, 0:1], halo[0][:, 0:1]))
    C.g.wait_ge(C.sC.h, cm)
    C.g.wait_ge(C.sV.h, C.sV.n)
    from concourse.bass_core_rust import IndirectOffsetOnAxis
    hxs = C.sbuf("s3_hidx", [128, ND], I32)
    C.dG.inc(C.g.dma_start(out=hxs[:, :], in_=hidx), 16)
    C.g.wait_ge(C.dG.h, C.dG.n)
    C.g.wait_ge(C.sV.h, C.sV.n)
    for i in range(ND):
        hh = C.g.indirect_dma_start(
            halo[i][:, :], None, d_gath,
            IndirectOffsetOnAxis(ap=hxs[:, i:i + 1], axis=0))
        C.dG.inc(hh, 16)
    hmask = C.sbuf("s3_hm", [128, 1], F32)
    C.dL.inc(C.sy.dma_start(out=hmask[:, :], in_=C.inp("hmask", [128, 1])), 16)
    C.v.wait_ge(C.dG.h, C.dG.n)
    C.v.wait_ge(C.dL.h, C.dL.n)
    for i in range(ND):
        h = C.v.tensor_scalar(halo[i][:, :], halo[i][:, :], hmask[:, :],
                              None, ALU.mult)
    C.sV.inc(h)
    for e in (C.v, C.a, C.t):
        e.wait_ge(C.dG.h, C.dG.n)
        e.wait_ge(C.sV.h, C.sV.n)

    kblk = [C.sbuf(f"s3_kb{j}", [128, D], BF16) for j in range(3)]
    vblk = [C.sbuf(f"s3_vb{j}", [128, D], BF16) for j in range(3)]
    qtt = C.sbuf("s3_qt", [128, D], BF16)
    attnT = C.sbuf("s3_at", [128, 3 * 128], BF16)
    oTt = C.sbuf("s3_ot", [128, D], BF16)
    osq = C.sbuf("s3_osq", [128, 128], BF16)
    orow = C.sbuf("s3_orow", [1, 2 * 128], F32)
    otmp = C.sbuf("s3_otmp", [128, 128], F32)

    def src_chunk(b, i):
        if b < 0:
            return halo[i][:, 128 * (b + 2):128 * (b + 3)]
        return xn[i][:, 128 * b:128 * (b + 1)]

    bmark = {}

    def bwait(r):
        if S3_COARSE:
            C.t.wait_ge(C.sV.h, C.sV.n)
        else:
            C.wait(C.t, bmark.get(r))

    def build_block(b, slot):
        for i in range(ND):
            bx = [0, 2][i % 2]
            bwait((bx, 0))
            for di in range(ND):
                h = C.t.matmul(pb[bx][:, 0:128],
                               wqkv[di][:, D + 128 * i:D + 128 * (i + 1)],
                               src_chunk(b, di), start=(di == 0), stop=(di == ND - 1))
            tm = C.sT.inc(h)
            C.v.wait_ge(C.sT.h, tm)
            vm = C.sV.inc(C.v.tensor_scalar(kblk[slot][:, 128 * i:128 * (i + 1)],
                                            pb[bx][:, 0:128],
                                            bqkv[:, 8 + i:9 + i], None, ALU.add))
            bmark[(bx, 0)] = (C.sV, vm)
        for eh in range(2):
            esl = slice(512 * eh, 512 * (eh + 1))
            bx = [1, 3][eh]
            bwait((bx, 0))
            for di in range(ND):
                C.t.matmul(pb[bx][:, :], src_chunk(b, di),
                           wqkv[di][:, 2 * D + 512 * eh:2 * D + 512 * (eh + 1)],
                           start=(di == 0), stop=False)
            h = C.t.matmul(pb[bx][:, :], onesb[:, :], bvr[:, esl],
                           start=False, stop=True)
            tm = C.sT.inc(h)
            C.v.wait_ge(C.sT.h, tm)
            vm = C.sV.inc(C.v.tensor_copy(vblk[slot][:, esl], pb[bx][:, :]))
            bmark[(bx, 0)] = (C.sV, vm)

    build_block(-2, 0)
    build_block(-1, 1)
    for t in range(NT):
        slot_t = (t + 2) % 3
        build_block(t, slot_t)
        qdone = None
        for i in range(ND):
            bx = [0, 2][i % 2]
            bwait((bx, 0))
            for di in range(ND):
                h = C.t.matmul(pb[bx][:, 0:128],
                               wqkv[di][:, 128 * i:128 * (i + 1)],
                               xn[di][:, 128 * t:128 * (t + 1)],
                               start=(di == 0), stop=(di == ND - 1))
            tm = C.sT.inc(h)
            C.v.wait_ge(C.sT.h, tm)
            qdone = C.sV.inc(C.v.tensor_scalar(qtt[:, 128 * i:128 * (i + 1)],
                                               pb[bx][:, 0:128],
                                               bqkv[:, i:i + 1], None, ALU.add))
            bmark[(bx, 0)] = (C.sV, qdone)
        C.t.wait_ge(C.sV.h, qdone)
        attndone = None
        for j in range(3):
            slot = (t + j) % 3
            bx = [1, 3][j % 2]
            bwait((bx, 0))
            for i in range(ND):
                h = C.t.matmul(pb[bx][:, 0:128],
                               kblk[slot][:, 128 * i:128 * (i + 1)],
                               qtt[:, 128 * i:128 * (i + 1)],
                               start=(i == 0), stop=(i == ND - 1))
            tm = C.sT.inc(h)
            C.v.wait_ge(C.sT.h, tm)
            if j == 0:
                attndone = C.sV.inc(C.v.tensor_tensor(
                    attnT[:, 128 * j:128 * (j + 1)],
                    pb[bx][:, 0:128], m0T[:, :], ALU.mult))
            elif j == 1:
                attndone = C.sV.inc(C.v.tensor_scalar(
                    attnT[:, 128 * j:128 * (j + 1)],
                    pb[bx][:, 0:128], 1.0 / 512.0, None, ALU.mult))
            else:
                attndone = C.sV.inc(C.v.tensor_tensor(
                    attnT[:, 128 * j:128 * (j + 1)],
                    pb[bx][:, 0:128], m2T[:, :], ALU.mult))
            bmark[(bx, 0)] = (C.sV, attndone)
        C.t.wait_ge(C.sV.h, attndone)
        otdone = None
        for i in range(ND):
            bk, off = 4 + (i % 2), 0
            bwait((bk, off))
            for j in range(3):
                slot = (t + j) % 3
                h = C.t.matmul(pb[bk][:, off:off + 128],
                               vblk[slot][:, 128 * i:128 * (i + 1)],
                               attnT[:, 128 * j:128 * (j + 1)],
                               start=(j == 0), stop=(j == 2))
            tm = C.sT.inc(h)
            C.v.wait_ge(C.sT.h, tm)
            otdone = C.sV.inc(C.v.tensor_copy(oTt[:, 128 * i:128 * (i + 1)],
                                              pb[bk][:, off:off + 128]))
            bmark[(bk, off)] = (C.sV, otdone)
        bwait((6, 0))
        C.t.wait_ge(C.sV.h, otdone)
        for i in range(ND):
            C.t.matmul(pb[6][0:1, 0:128], C.sb["onescb"][:, :],
                       oTt[:, 128 * i:128 * (i + 1)],
                       start=(i == 0), stop=(i == ND - 1))
        for i in range(ND):
            vm = C.sV.inc(C.v.tensor_mul(osq[:, :], oTt[:, 128 * i:128 * (i + 1)],
                                         oTt[:, 128 * i:128 * (i + 1)]))
            C.t.wait_ge(C.sV.h, vm)
            tm = C.sT.inc(C.t.matmul(pb[6][0:1, 128:256], C.sb["onescb"][:, :],
                                     osq[:, :], start=(i == 0), stop=(i == ND - 1)))
            C.v.wait_ge(C.sT.h, tm)
        C.v.wait_ge(C.sT.h, C.sT.n)
        vm = C.sV.inc(C.v.tensor_scalar(orow[0:1, 0:128], pb[6][0:1, 0:128],
                                        1.0 / 1024.0, None, ALU.mult))
        C.v.wait_ge(C.sV.h, vm)
        vm = C.sV.inc(C.v.tensor_mul(otmp[0:1, 0:128], orow[0:1, 0:128],
                                     orow[0:1, 0:128]))
        C.v.wait_ge(C.sV.h, vm)
        vm = C.sV.inc(C.v.tensor_scalar(orow[0:1, 128:256], pb[6][0:1, 128:256],
                                        1.0 / 1024.0, 1e-5, ALU.mult, ALU.add))
        bmark[(6, 0)] = (C.sV, vm)
        C.v.wait_ge(C.sV.h, vm)
        vm = C.sV.inc(C.v.tensor_sub(orow[0:1, 128:256], orow[0:1, 128:256],
                                     otmp[0:1, 0:128]))
        C.a.wait_ge(C.sV.h, vm)
        am = C.sA.inc(C.a.activation(orow[0:1, 128:256], orow[0:1, 128:256], AF.Ln))
        C.a.wait_ge(C.sA.h, am)
        am = C.sA.inc(C.a.activation(orow[0:1, 128:256], orow[0:1, 128:256],
                                     AF.Exp, scale=-0.5))
        C.v.wait_ge(C.sA.h, am)
        vm = C.sV.inc(C.v.scalar_tensor_tensor(orow[0:1, 0:128], orow[0:1, 0:128],
                                               -1.0, orow[0:1, 128:256],
                                               ALU.mult, ALU.mult))
        C.t.wait_ge(C.sV.h, vm)
        bwait((7, 0))
        C.t.matmul(pb[7][:, 0:128], onesf[:, :], orow[0:1, 128:256],
                   start=True, stop=True)
        tm = C.sT.inc(C.t.matmul(pb[7][:, 128:256], onesf[:, :], orow[0:1, 0:128],
                                 start=True, stop=True))
        C.v.wait_ge(C.sT.h, tm)
        lastv = None
        for i in range(ND):
            tsl = slice(128 * t, 128 * (t + 1))
            vm = C.sV.inc(C.v.tensor_tensor(otmp[:, :], oTt[:, 128 * i:128 * (i + 1)],
                                            pb[7][:, 0:128], ALU.mult))
            C.v.wait_ge(C.sV.h, vm)
            vm = C.sV.inc(C.v.tensor_tensor(otmp[:, :], otmp[:, :],
                                            pb[7][:, 128:256], ALU.add))
            C.v.wait_ge(C.sV.h, vm)
            vm = C.sV.inc(C.v.tensor_scalar(otmp[:, :], otmp[:, :],
                                            lnw[:, i:i + 1], lnb[:, i:i + 1],
                                            ALU.mult, ALU.add))
            C.v.wait_ge(C.sV.h, vm)
            lastv = C.sV.inc(C.v.tensor_add(xT[i][:, tsl], xT[i][:, tsl],
                                            otmp[:, :]))
        bmark[(7, 0)] = (C.sV, lastv)
    return (C.sV, C.sV.n)


def _register():
    EMITTERS[2] = lambda C, xT, mark: emit_s2(C, xT, mark, "s2")
    EMITTERS[7] = lambda C, xT, mark: emit_s7(C, xT, mark)
    EMITTERS[6] = lambda C, xT, mark: emit_s6(C, xT, mark, "s6")
    EMITTERS[3] = lambda C, xT, mark: emit_s3(C, xT, mark)
    EMITTERS[8] = lambda C, xT, mark: emit_s6(C, xT, mark, "s8")
    EMITTERS[4] = lambda C, xT, mark: emit_s2(C, xT, mark, "s4")


_register()


def emit_s5(C, xT, mark, tag="s5"):
    pb = get_pb(C)
    xn = [C.sbuf(f"xn{i}", [128, T], BF16) for i in range(ND)]
    g1 = C.sbuf("s5_g1", [128, ND], F32)
    be1 = C.sbuf("s5_be1", [128, ND], F32)
    C.dL.inc(C.sy.dma_start(out=g1[:, :], in_=C.inp("g1c", [128, ND])), 16)
    C.dL.inc(C.sy.dma_start(out=be1[:, :], in_=C.inp("be1c", [128, ND])), 16)
    C.v.wait_ge(C.dL.h, C.dL.n)
    mark = emit_ln(C, xT, xn, mark)
    d_xn = C.scratch(f"{tag}_xn", [D, T], BF16)
    C.wait(C.sy, mark)
    for i in range(ND):
        C.dL.inc(C.sy.dma_start(out=d_xn[128 * i:128 * (i + 1), :],
                                in_=xn[i][:, :]), 16)
    spillm = C.dL.n
    for e in (C.v, C.a, C.t):
        e.wait_ge(C.dL.h, spillm)
    mark = emit_ln(C, xn, xn, mark, gamma="s5_g1", beta="s5_be1",
                   eps=1e-6, f32src=False)
    ln1 = xn

    pool = [C.sbuf(f"s5_wp{i}", [128, D], BF16) for i in range(2 * ND)]
    qT = [C.sbuf(f"s5_qT{i}", [128, T], BF16) for i in range(ND)]
    kt = C.sbuf("s5_kt", [128, D], BF16)
    vt = C.sbuf("s5_vt", [128, D], BF16)
    kvsS = C.sbuf("s5_kvs", [64, 1040], F32)
    kvb = C.sbuf("s5_kvb", [128, 1040], BF16)
    rd = C.sbuf("s5_rd", [128, 1], F32)
    b1c = C.sbuf("s5_b1c", [128, 32], F32)
    b2c = C.sbuf("s5_b2c", [128, ND], F32)
    C.dL.inc(C.sy.dma_start(out=b1c[:, :], in_=C.inp("b1c", [128, 32])), 16)
    C.dL.inc(C.sy.dma_start(out=b2c[:, :], in_=C.inp("b2c", [128, ND])), 16)
    d_kv = C.scratch(f"{tag}_kv", [64, 1040])
    d_kvr = C.scratch(f"{tag}_kvr", [64, 1040])

    wqf = C.inp("wqf", [D, D])
    for i in range(ND):
        C.dG.inc(C.g.dma_start(out=pool[i][:, :],
                               in_=wqf[128 * i:128 * (i + 1), :]), 16)
    C.wait(C.t, mark)
    C.t.wait_ge(C.dG.h, C.dG.n)
    C.t.wait_ge(C.sA.h, C.sA.n)
    C.t.wait_ge(C.sV.h, C.sV.n)
    amk = {}
    for fi in range(ND):
        for qc in range(4):
            bx = qc % 2
            bank = pb[bx]
            s_ = slice(512 * qc, 512 * (qc + 1))
            C.wait(C.t, amk.get(bx))
            for di in range(ND):
                h = C.t.matmul(bank[:, :], pool[di][:, 128 * fi:128 * (fi + 1)],
                               ln1[di][:, s_], start=(di == 0), stop=(di == ND - 1))
            tm = C.sT.inc(h)
            C.a.wait_ge(C.sT.h, tm)
            am = C.sA.inc(C.a.activation(qT[fi][:, s_], bank[:, :], AF.Relu))
            amk[bx] = (C.sA, am)
    C.v.wait_ge(C.sA.h, C.sA.n)
    for fi in range(ND):
        h = C.v.tensor_scalar(qT[fi][:, :], qT[fi][:, :], 0.001, None, ALU.add)
    qpm = C.sV.inc(h)

    wkf = C.inp("wkf", [D, D])
    wvf = C.inp("wvf", [D, D])
    C.g.wait_ge(C.sT.h, C.sT.n)
    for i in range(ND):
        C.dG.inc(C.g.dma_start(out=pool[i][:, :],
                               in_=wkf[128 * i:128 * (i + 1), :]), 16)
        C.dG.inc(C.g.dma_start(out=pool[ND + i][:, :],
                               in_=wvf[128 * i:128 * (i + 1), :]), 16)
    C.t.wait_ge(C.dG.h, C.dG.n)
    C.t.wait_ge(C.sV.h, qpm)
    lastkv = None
    pmk = {}
    for lt in range(NT):
        tsl = slice(128 * lt, 128 * (lt + 1))
        for eh in range(2):
            esl = slice(512 * eh, 512 * (eh + 1))
            C.wait(C.t, pmk.get(0))
            for di in range(ND):
                h = C.t.matmul(pb[0][:, :], ln1[di][:, tsl], pool[di][:, esl],
                               start=(di == 0), stop=(di == ND - 1))
            tm = C.sT.inc(h)
            C.a.wait_ge(C.sT.h, tm)
            am = C.sA.inc(C.a.activation(kt[:, esl], pb[0][:, :], AF.Relu))
            pmk[0] = (C.sA, am)
            C.wait(C.t, pmk.get(1))
            for di in range(ND):
                h = C.t.matmul(pb[1][:, :], ln1[di][:, tsl],
                               pool[ND + di][:, esl],
                               start=(di == 0), stop=(di == ND - 1))
            tm = C.sT.inc(h)
            C.v.wait_ge(C.sT.h, tm)
            vm = C.sV.inc(C.v.tensor_copy(vt[:, esl], pb[1][:, :]))
            pmk[1] = (C.sV, vm)
        C.v.wait_ge(C.sA.h, C.sA.n)
        vm = C.sV.inc(C.v.tensor_scalar(kt[:, :], kt[:, :], 0.001, None, ALU.add))
        C.t.wait_ge(C.sV.h, vm)
        for hh in range(16):
            bank = pb[2 + hh // 8]
            hs = slice(64 * hh, 64 * (hh + 1))
            lastkv = C.t.matmul(bank[0:64, 64 * (hh % 8):64 * (hh % 8) + 64],
                                kt[:, hs], vt[:, hs],
                                start=(lt == 0 and hh % 8 == 0),
                                stop=(lt == NT - 1 and hh % 8 == 7))
            lastks = C.t.matmul(pb[4][0:64, hh:hh + 1], kt[:, hs],
                                C.sb["onescb"][:, :],
                                start=(lt == 0 and hh == 0),
                                stop=(lt == NT - 1 and hh == 15))
    tm = C.sT.inc(lastks)
    C.sT.inc(lastkv)
    C.v.wait_ge(C.sT.h, C.sT.n)
    C.v.tensor_copy(kvsS[:, 0:512], pb[2][0:64, 0:512])
    C.v.tensor_copy(kvsS[:, 512:1024], pb[3][0:64, 0:512])
    vm = C.sV.inc(C.v.tensor_copy(kvsS[:, 1024:1040], pb[4][0:64, 0:16]))
    C.wait(C.sy, (C.sV, vm))
    C.dL.inc(C.sy.dma_start(out=d_kv, in_=kvsS[:, :]), 16)
    C.g.wait_ge(C.dL.h, C.dL.n)
    cm = C.sC.inc(C.g.collective_compute(
        "AllReduce", ALU.add, replica_groups=[[0, 1, 2, 3], [4, 5, 6, 7]],
        ins=[d_kv.opt()], outs=[d_kvr.opt()]))
    C.sy.wait_ge(C.sC.h, cm)
    C.dL.inc(C.sy.dma_start(out=kvsS[:, :], in_=d_kvr), 16)
    C.v.wait_ge(C.dL.h, C.dL.n)
    C.v.tensor_copy(kvb[0:64, :], kvsS[:, :])
    vm = C.sV.inc(C.v.tensor_copy(kvb[64:128, :], kvsS[:, :]))

    C.t.wait_ge(C.sV.h, vm)
    cmk = {}
    it = 0
    for hh in range(16):
        fi = hh // 2
        r0 = 64 * (hh % 2)
        for lt in range(NT):
            tsl = slice(128 * lt, 128 * (lt + 1))
            ba, bt = (5, 6) if it % 2 == 0 else (0, 1)
            it += 1
            C.wait(C.t, cmk.get(ba))
            C.t.matmul(pb[ba][:, 0:64], qT[fi][r0:r0 + 64, tsl],
                       kvb[r0:r0 + 64, 64 * hh:64 * hh + 64],
                       start=True, stop=False)
            h = C.t.matmul(pb[ba][:, 64:65], qT[fi][r0:r0 + 64, tsl],
                           kvb[r0:r0 + 64, 1024 + hh:1025 + hh],
                           start=False, stop=True)
            tm = C.sT.inc(h)
            C.v.wait_ge(C.sT.h, tm)
            vm = C.sV.inc(C.v.reciprocal(rd[:, :], pb[ba][:, 64:65]))
            C.v.wait_ge(C.sV.h, vm)
            vm = C.sV.inc(C.v.tensor_scalar(kt[:, 0:64], pb[ba][:, 0:64],
                                            rd[:, :], None, ALU.mult))
            cmk[ba] = (C.sV, vm)
            C.wait(C.t, cmk.get(bt))
            C.t.wait_ge(C.sV.h, vm)
            tm = C.sT.inc(C.t.transpose(pb[bt][:, 0:64].bitcast(BF16)[0:64, 0:128],
                                        kt[:, 0:64], C.sb["idb"][:, :]))
            C.v.wait_ge(C.sT.h, tm)
            vm = C.sV.inc(C.v.tensor_copy(qT[fi][r0:r0 + 64, tsl],
                                          pb[bt][:, 0:64].bitcast(BF16)[0:64, 0:128]))
            cmk[bt] = (C.sV, vm)
    attm = (C.sV, C.sV.n)

    wof = C.inp("wof", [D, D])
    C.g.wait_ge(C.sT.h, C.sT.n)
    for i in range(ND):
        C.dG.inc(C.g.dma_start(out=pool[i][:, :],
                               in_=wof[128 * i:128 * (i + 1), :]), 16)
    C.wait(C.sy, attm)
    for i in range(ND):
        C.dL.inc(C.sy.dma_start(out=xn[i][:, :],
                                in_=d_xn[128 * i:128 * (i + 1), :]), 16)
    C.wait(C.t, attm)
    C.t.wait_ge(C.dG.h, C.dG.n)
    for e in (C.v, C.a, C.t):
        e.wait_ge(C.dL.h, C.dL.n)
    dmk = {}
    for ei in range(ND):
        for qc in range(4):
            bx = qc % 2
            bank = pb[bx]
            s_ = slice(512 * qc, 512 * (qc + 1))
            C.wait(C.t, dmk.get(bx))
            for di in range(ND):
                h = C.t.matmul(bank[:, :], pool[di][:, 128 * ei:128 * (ei + 1)],
                               qT[di][:, s_], start=(di == 0), stop=(di == ND - 1))
            tm = C.sT.inc(h)
            C.v.wait_ge(C.sT.h, tm)
            vm = C.sV.inc(C.v.tensor_add(xn[ei][:, s_], xn[ei][:, s_],
                                         bank[:, :]))
            dmk[bx] = (C.sV, vm)
            C.v.wait_ge(C.sV.h, vm)
            C.sV.inc(C.v.tensor_add(xT[ei][:, s_], xT[ei][:, s_],
                                    xn[ei][:, s_]))
    mark = (C.sV, C.sV.n)

    mark = emit_ln(C, xn, xn, mark, gamma="s5_g1", beta="s5_be1",
                   eps=1e-6, f32src=False)

    w1i = C.inp("w1i", [32, 128, 8, 128])
    w2i = C.inp("w2i", [8, 128, 32, 128])
    w14 = [C.sbuf(f"s5_w14_{j}", [128, 4096], BF16, alias="scrA",
                  off=8192 * j) for j in range(2)]
    C.wait(C.t, mark)
    C.wait(C.g, mark)
    wld = [None, None]
    fmk = {}
    ktm = {}
    for tc in range(4):
        s_ = slice(512 * tc, 512 * (tc + 1))
        for g4 in range(8):
            j = g4 % 2
            C.wait(C.g, wld[j])
            C.dG.inc(C.g.dma_start(
                out=w14[j][:, :],
                in_=bass.AP(w1i.tensor, 4 * g4 * 128 * 1024,
                            [[1024, 128], [128 * 1024, 4], [1, 1024]])), 16)
            ldw = C.dG.n
            C.t.wait_ge(C.dG.h, ldw)
            for q4 in range(4):
                fp = 4 * g4 + q4
                h1slot = qT[fp // 4][:, 512 * (fp % 4):512 * (fp % 4 + 1)]
                bx = fp % 2
                bank = pb[bx]
                C.wait(C.t, fmk.get(bx))
                for di in range(ND):
                    h = C.t.matmul(bank[:, :],
                                   w14[j][:, 1024 * q4 + 128 * di:
                                          1024 * q4 + 128 * (di + 1)],
                                   ln1[di][:, s_], start=(di == 0),
                                   stop=(di == ND - 1))
                tm = C.sT.inc(h)
                C.a.wait_ge(C.sT.h, tm)
                am = C.sA.inc(C.a.activation(h1slot, bank[:, :], AF.Gelu,
                                             bias=b1c[:, fp:fp + 1]))
                fmk[bx] = (C.sA, am)
            wld[j] = (C.sT, C.sT.n)
        C.wait(C.t, fmk.get(0))
        C.wait(C.t, fmk.get(1))
        for ei in range(ND):
            j = ei % 2
            C.wait(C.g, wld[j])
            C.dG.inc(C.g.dma_start(
                out=w14[j][:, :],
                in_=bass.AP(w2i.tensor, ei * 128 * 4096,
                            [[4096, 128], [1, 4096]])), 16)
            ldw = C.dG.n
            C.t.wait_ge(C.dG.h, ldw)
            bx = ei % 2
            bank = pb[bx]
            ks = kt[:, 512 * bx:512 * (bx + 1)]
            C.wait(C.t, fmk.get(bx))
            for fp in range(32):
                h1slot = qT[fp // 4][:, 512 * (fp % 4):512 * (fp % 4 + 1)]
                h = C.t.matmul(bank[:, :], w14[j][:, 128 * fp:128 * (fp + 1)],
                               h1slot, start=(fp == 0), stop=(fp == 31))
            tm = C.sT.inc(h)
            wld[j] = (C.sT, C.sT.n)
            C.a.wait_ge(C.sT.h, tm)
            C.wait(C.a, ktm.get(bx))
            am = C.sA.inc(C.a.activation(ks, bank[:, :], AF.Gelu,
                                         bias=b2c[:, ei:ei + 1]))
            fmk[bx] = (C.sA, am)
            C.v.wait_ge(C.sA.h, am)
            vm = C.sV.inc(C.v.tensor_add(xT[ei][:, s_], xT[ei][:, s_], ks))
            ktm[bx] = (C.sV, vm)
    return (C.sV, C.sV.n)


def _register():
    EMITTERS[2] = lambda C, xT, mark: emit_s2(C, xT, mark, "s2")
    EMITTERS[7] = lambda C, xT, mark: emit_s7(C, xT, mark)
    EMITTERS[6] = lambda C, xT, mark: emit_s6(C, xT, mark, "s6")
    EMITTERS[3] = lambda C, xT, mark: emit_s3(C, xT, mark)
    EMITTERS[8] = lambda C, xT, mark: emit_s6(C, xT, mark, "s8")
    EMITTERS[4] = lambda C, xT, mark: emit_s2(C, xT, mark, "s4")
    EMITTERS[5] = lambda C, xT, mark: emit_s5(C, xT, mark)
    EMITTERS[1] = lambda C, xT, mark: emit_s1(C, xT, mark)


_register()


S1_DEBUG = False
S3_COARSE = False
S3_MONO = True


def emit_s1(C, xT, mark, tag="s1"):
    pb = get_pb(C)
    onesf = C.sb["onesf"]
    onesb = C.sb["onesb"]
    xn = [C.sbuf(f"xn{i}", [128, T], BF16) for i in range(ND)]
    mark = emit_ln(C, xT, xn, mark)

    pibuf = [C.sbuf(f"s1_pi{i}", [128, T], BF16) for i in range(ND)]
    qT = [C.sbuf(f"s1_qT{i}", [128, T], BF16) for i in range(ND)]
    pool = [C.sbuf(f"s1_wp{i}", [128, D], BF16) for i in range(ND)]
    sc2 = C.sbuf("s1_sc2", [128, T], BF16)
    binq = C.sbuf("s1_bq", [128, ND], F32)
    bink = C.sbuf("s1_bk", [128, ND], F32)
    bout = C.sbuf("s1_bo", [128, ND], F32)
    binvr = C.sbuf("s1_bvr", [1, D], BF16)
    C.dL.inc(C.sy.dma_start(out=binq[:, :], in_=C.inp("binq_c", [128, ND])), 16)
    C.dL.inc(C.sy.dma_start(out=bink[:, :], in_=C.inp("bink_c", [128, ND])), 16)
    C.dL.inc(C.sy.dma_start(out=bout[:, :], in_=C.inp("bout_c", [128, ND])), 16)
    C.dG.inc(C.g.dma_start(out=binvr[:, :], in_=C.inp("binv_r", [1, D])), 16)

    d_kcon = C.scratch(f"{tag}_kcon", [D, T], BF16)
    d_vcon = C.scratch(f"{tag}_vcon", [T, 1040], BF16)
    d_kg = C.scratch(f"{tag}_kg", [4 * D, T], BF16)
    d_vg = C.scratch(f"{tag}_vg", [4 * T, 1040], BF16)
    d_expE = C.scratch(f"{tag}_expE", [16 * 16000], BF16)
    d_expb = C.scratch(f"{tag}_expb", [64 * 128 * 31 * 128], BF16)
    epad = C.inp("epad", [16, 16000])

    C.wait(C.v, mark)
    for i in range(ND):
        h = C.v.tensor_copy(
            pibuf[i][:, :].rearrange("p (r m) -> p r m", r=16),
            xn[i][:, :].rearrange("p (m r) -> p r m", r=16))
    vm = C.sV.inc(h)
    C.v.wait_ge(C.sV.h, vm)
    for i in range(ND):
        h = C.v.tensor_copy(
            xn[i][:, :].rearrange("p (r u) -> p r u", r=16),
            pibuf[i][:, :].rearrange("p (r u) -> p r u", r=16)[:, :, ::-1])
    phim = C.sV.inc(h)
    phi = xn
    if S1_DEBUG:
        C.wait(C.sy, (C.sV, phim))
        C.dS.inc(C.sy.dma_start(out=C.out("dbg_phi0", [128, T], BF16),
                                in_=xn[0][:, :]), 16)
        C.dS.inc(C.sy.dma_start(out=C.out("dbg_pi0", [128, T], BF16),
                                in_=pibuf[0][:, :]), 16)
        C.sy.wait_ge(C.dS.h, C.dS.n)

    ex1 = sc2[:, :].bitcast(F32)[:, 0:1000]
    ex2 = qT[0][:, 0:1000]
    for hf in range(2):
        epflat = bass.AP(epad.tensor, 1000 * hf, [[2000, 128], [1, 1000]])
        C.sy.wait_ge(C.dL.h, C.dL.n)
        C.wait(C.sy, (C.sA, C.sA.n))
        C.dL.inc(C.sy.dma_start(out=ex1, in_=epflat), 16)
        exm = C.dL.n
        C.a.wait_ge(C.dL.h, exm)
        am = C.sA.inc(C.a.activation(ex2, ex1, AF.Exp))
        C.wait(C.sy, (C.sA, am))
        C.dL.inc(C.sy.dma_start(
            out=bass.AP(d_expE.tensor, 1000 * hf, [[2000, 128], [1, 1000]]),
            in_=ex2), 16)
    C.sy.wait_ge(C.dL.h, C.dL.n)
    p4a_mark = (C.sA, C.sA.n)
    p4a_dma = (C.dL, C.dL.n)
    for kc in range(64):
        cbk, rk = kc // 16, kc % 16
        for g in range(4):
            nd31 = 8 if g < 3 else 7
            src = bass.AP(d_expE.tensor,
                          16000 * rk + 128 * cbk + 512 * (8 * g),
                          [[1, 128], [512, nd31], [1, 128]])
            dst = bass.AP(d_expb.tensor,
                          kc * 128 * 3968 + 128 * (8 * g),
                          [[3968, 128], [128, nd31], [1, 128]])
            C.dL.inc(C.sy.dma_start(out=dst, in_=src), 16)
    prepm = C.dL.n

    wqd = C.inp("Winq", [D, D])
    for i in range(ND):
        C.dG.inc(C.g.dma_start(out=pool[i][:, :],
                               in_=wqd[128 * i:128 * (i + 1), :]), 16)
    C.wait(C.t, (C.sV, phim))
    C.t.wait_ge(C.dG.h, C.dG.n)
    C.v.wait_ge(C.sA.h, p4a_mark[1])
    C.v.wait_ge(C.dL.h, p4a_dma[1])
    C.t.wait_ge(C.sV.h, C.sV.n)
    C.t.wait_ge(C.sA.h, C.sA.n)
    pmk = {}
    for fi in range(ND):
        for qc in range(4):
            bx = qc % 2
            bank = pb[bx]
            s_ = slice(512 * qc, 512 * (qc + 1))
            C.wait(C.t, pmk.get(bx))
            for di in range(ND):
                h = C.t.matmul(bank[:, :], pool[di][:, 128 * fi:128 * (fi + 1)],
                               phi[di][:, s_], start=(di == 0), stop=(di == ND - 1))
            tm = C.sT.inc(h)
            C.v.wait_ge(C.sT.h, tm)
            vm = C.sV.inc(C.v.tensor_scalar(qT[fi][:, s_], bank[:, :],
                                            binq[:, fi:fi + 1], 0.125,
                                            ALU.add, ALU.mult))
            pmk[bx] = (C.sV, vm)
    wkd = C.inp("Wink", [D, D])
    C.g.wait_ge(C.sT.h, C.sT.n)
    for i in range(ND):
        C.dG.inc(C.g.dma_start(out=pool[i][:, :],
                               in_=wkd[128 * i:128 * (i + 1), :]), 16)
    C.t.wait_ge(C.dG.h, C.dG.n)
    stk = None
    for fi in range(ND):
        C.wait(C.v, stk)
        for qc in range(4):
            bx = qc % 2
            bank = pb[bx]
            s_ = slice(512 * qc, 512 * (qc + 1))
            C.wait(C.t, pmk.get(bx))
            for di in range(ND):
                h = C.t.matmul(bank[:, :], pool[di][:, 128 * fi:128 * (fi + 1)],
                               pibuf[di][:, s_], start=(di == 0), stop=(di == ND - 1))
            tm = C.sT.inc(h)
            C.v.wait_ge(C.sT.h, tm)
            vm = C.sV.inc(C.v.tensor_scalar(sc2[:, s_], bank[:, :],
                                            bink[:, fi:fi + 1], None, ALU.add))
            pmk[bx] = (C.sV, vm)
        C.wait(C.sy, (C.sV, C.sV.n))
        C.dL.inc(C.sy.dma_start(out=d_kcon[128 * fi:128 * (fi + 1), :],
                                in_=sc2[:, :]), 16)
        stk = (C.dL, C.dL.n)
    wvd = C.inp("Winv", [D, D])
    C.g.wait_ge(C.sT.h, C.sT.n)
    for i in range(ND):
        C.dG.inc(C.g.dma_start(out=pool[i][:, :],
                               in_=wvd[128 * i:128 * (i + 1), :]), 16)
    C.t.wait_ge(C.dG.h, C.dG.n)
    C.v.wait_ge(C.dL.h, C.dL.n)
    vo = C.sV.inc(C.v.memset(sc2[:, :], 1.0))
    C.t.wait_ge(C.sV.h, vo)
    stv = None
    for hh in range(16):
        C.wait(C.v, stv)
        for ti in range(NT):
            bx = ti % 2
            bank = pb[bx]
            tsl = slice(128 * ti, 128 * (ti + 1))
            C.wait(C.t, pmk.get(bx))
            for di in range(ND):
                C.t.matmul(bank[:, 0:64], pibuf[di][:, tsl],
                           pool[di][:, 64 * hh:64 * (hh + 1)],
                           start=(di == 0), stop=False)
            h = C.t.matmul(bank[:, 0:64], onesb[:, :],
                           binvr[:, 64 * hh:64 * (hh + 1)], start=False, stop=True)
            tm = C.sT.inc(h)
            C.v.wait_ge(C.sT.h, tm)
            vm = C.sV.inc(C.v.tensor_copy(sc2[:, 65 * ti:65 * ti + 64],
                                          bank[:, 0:64]))
            pmk[bx] = (C.sV, vm)
        C.wait(C.sy, (C.sV, C.sV.n))
        C.dL.inc(C.sy.dma_start(
            out=bass.AP(d_vcon.tensor, 1040 * 128 * hh, [[1040, 128], [1, 1040]]),
            in_=sc2[:, 0:1040]), 16)
        stv = (C.dL, C.dL.n)
    conm = C.dL.n

    C.g.wait_ge(C.dL.h, conm)
    cm1 = C.sC.inc(C.g.collective_compute(
        "AllGather", ALU.bypass, replica_groups=[[0, 1, 2, 3], [4, 5, 6, 7]],
        ins=[d_kcon.opt()], outs=[d_kg.opt()]))
    cm2 = C.sC.inc(C.g.collective_compute(
        "AllGather", ALU.bypass, replica_groups=[[0, 1, 2, 3], [4, 5, 6, 7]],
        ins=[d_vcon.opt()], outs=[d_vg.opt()]))
    for e in (C.sy, C.t, C.v, C.a, C.g):
        e.wait_ge(C.sC.h, C.sC.n)
    C.sy.wait_ge(C.dL.h, prepm)

    NEB = 4
    eb = [C.sbuf(f"s1_e{j}", [128, 2048], BF16, alias="scrA",
                 off=4096 * j) for j in range(NEB)]
    exps2 = [xn[0], sc2]
    rrow = binvr[0:1, :].bitcast(F32)[0:1, 0:512]
    aon = pibuf
    pSh = [pb[8][:, 0:1024], pb[8][:, 1024:2048]]
    pA = [pb[4 + qc] for qc in range(4)]
    ebm = [None] * NEB
    ebl = [0] * NEB
    avx = [None, None]
    avk = [None, None]
    avv = [None, None]
    exph = [None, None]
    scm = {}

    def emit_loads(hh, kc):
        cbk, rk = kc // 16, kc % 16
        j4 = (kc // 4) % 2
        jv = (kc // 16) % 2
        jb = kc % NEB
        if rk == 0:
            C.wait(C.g, avv[jv])
            C.dG.inc(C.g.dma_start(
                out=xn[3 + jv][:, 0:1040],
                in_=d_vg[2048 * cbk + 128 * hh:2048 * cbk + 128 * (hh + 1), :]), 16)
        if rk % 4 == 0:
            C.wait(C.g, avk[j4])
            r0 = 64 * (hh % 2)
            C.dG.inc(C.g.dma_start(
                out=xn[1 + j4][r0:r0 + 64, 0:512],
                in_=d_kg[1024 * cbk + 64 * hh:1024 * cbk + 64 * (hh + 1),
                         128 * rk:128 * (rk + 4)]), 16)
        C.wait(C.sy, ebm[jb])
        C.dL.inc(C.sy.dma_start(
            out=eb[jb][:, :],
            in_=bass.AP(d_expb.tensor, kc * 128 * 3968 + 128 * (15 - hh),
                        [[3968, 128], [1, 2048]])), 16)
        ebl[jb] = C.dL.n
        return C.dL.n, C.dG.n

    def emit_scores(hh, kc, ldg):
        fi, r0 = hh // 2, 64 * (hh % 2)
        rk = kc % 16
        j4 = (kc // 4) % 2
        lhs = xn[1 + j4][r0:r0 + 64, 128 * (rk % 4):128 * (rk % 4) + 128]
        C.t.wait_ge(C.dG.h, ldg)
        C.wait(C.t, exph[0])
        for qc in (0, 1):
            h = C.t.matmul(pb[qc][:, :], lhs, qT[fi][r0:r0 + 64,
                           512 * qc:512 * (qc + 1)], start=True, stop=True)
        tmA = C.sT.inc(h)
        C.wait(C.t, exph[1])
        for qc in (2, 3):
            h = C.t.matmul(pb[qc][:, :], lhs, qT[fi][r0:r0 + 64,
                           512 * qc:512 * (qc + 1)], start=True, stop=True)
        scm[kc] = (tmA, C.sT.inc(h))

    ld0 = emit_loads(0, 0)
    emit_scores(0, 0, ld0[1])
    lds = {1: None}
    for hh in range(16):
        fi, r0 = hh // 2, 64 * (hh % 2)
        for kc in range(64):
            jv = (kc // 16) % 2
            je = kc % 2
            jb = kc % NEB
            rk = kc % 16
            nxt = (hh, kc + 1) if kc + 1 < 64 else ((hh + 1, 0) if hh + 1 < 16 else None)
            if nxt is not None:
                ldn = emit_loads(*nxt)
            tmA, tmB = scm.pop(kc)
            qmarks = []
            for qc in range(4):
                C.a.wait_ge(C.sT.h, tmA if qc < 2 else tmB)
                if qc == 0:
                    C.wait(C.a, avx[je])
                am = C.sA.inc(C.a.activation(
                    exps2[je][:, 512 * qc:512 * (qc + 1)], pb[qc][:, :], AF.Exp))
                exph[qc // 2] = (C.sA, am)
                C.v.wait_ge(C.sA.h, am)
                if qc == 0:
                    C.v.wait_ge(C.dL.h, ebl[jb])
                    C.wait(C.v, avx[je])
                vm = C.sV.inc(C.v.tensor_mul(
                    exps2[je][:, 512 * qc:512 * (qc + 1)],
                    exps2[je][:, 512 * qc:512 * (qc + 1)],
                    eb[jb][:, 512 * qc:512 * (qc + 1)]))
                qmarks.append(vm)
            ebm[jb] = (C.sV, qmarks[-1])
            if nxt is not None and nxt[1] != 0:
                emit_scores(nxt[0], nxt[1], ldn[1])
            for qc in range(4):
                C.t.wait_ge(C.sV.h, qmarks[qc])
                h = C.t.matmul(pA[qc][0:65, :],
                               xn[3 + jv][:, 65 * rk:65 * rk + 65],
                               exps2[je][:, 512 * qc:512 * (qc + 1)],
                               start=(kc == 0), stop=(kc == 63))
            mk = (C.sT, C.sT.inc(h))
            avx[je] = mk
            if rk == 15:
                avv[jv] = mk
            if rk % 4 == 3:
                avk[(kc // 4) % 2] = mk
        for qc in range(4):
            s_ = slice(512 * qc, 512 * (qc + 1))
            C.v.wait_ge(C.sT.h, C.sT.n)
            vm = C.sV.inc(C.v.tensor_copy(rrow[:, :], pA[qc][64:65, :]))
            C.v.wait_ge(C.sV.h, vm)
            vm = C.sV.inc(C.v.reciprocal(rrow[:, :], rrow[:, :]))
            C.t.wait_ge(C.sV.h, vm)
            C.wait(C.t, exph[0])
            C.wait(C.t, exph[1])
            tm = C.sT.inc(C.t.matmul(pb[0][0:64, 0:512], onesf[:, 0:64],
                                     rrow[:, :], start=True, stop=True))
            C.v.wait_ge(C.sT.h, tm)
            vm = C.sV.inc(C.v.tensor_copy(aon[fi][r0:r0 + 64, s_],
                                          pA[qc][0:64, :]))
            C.v.wait_ge(C.sV.h, vm)
            C.sV.inc(C.v.tensor_tensor(aon[fi][r0:r0 + 64, s_],
                                       aon[fi][r0:r0 + 64, s_],
                                       pb[0][0:64, 0:512], ALU.mult))
        exph = [None, None]
        C.t.wait_ge(C.sV.h, C.sV.n)
        if hh + 1 < 16:
            emit_scores(hh + 1, 0, C.dG.n)

    if S1_DEBUG:
        C.wait(C.sy, (C.sV, C.sV.n))
        C.dS.inc(C.sy.dma_start(out=C.out("dbg_qT0", [128, T], BF16), in_=qT[0][:, :]), 16)
        C.dS.inc(C.sy.dma_start(out=C.out("dbg_aon0", [128, T], BF16), in_=aon[0][:, :]), 16)
        C.sy.wait_ge(C.dS.h, C.dS.n)
    wod = C.inp("Wout", [D, D])
    C.g.wait_ge(C.sT.h, C.sT.n)
    for i in range(ND):
        C.dG.inc(C.g.dma_start(out=pool[i][:, :],
                               in_=wod[128 * i:128 * (i + 1), :]), 16)
    C.t.wait_ge(C.dG.h, C.dG.n)
    C.t.wait_ge(C.sV.h, C.sV.n)
    ophi = eb[0]
    opi = eb[1]
    for ei in range(ND):
        for qc in range(4):
            bank = pb[0] if qc % 2 == 0 else pb[1]
            s_ = slice(512 * qc, 512 * (qc + 1))
            if not (ei == 0 and qc == 0):
                C.t.wait_ge(C.sV.h, C.sV.n)
            for di in range(ND):
                h = C.t.matmul(bank[:, :], pool[di][:, 128 * ei:128 * (ei + 1)],
                               aon[di][:, s_], start=(di == 0), stop=(di == ND - 1))
            tm = C.sT.inc(h)
            C.v.wait_ge(C.sT.h, tm)
            C.sV.inc(C.v.tensor_scalar(ophi[:, s_], bank[:, :],
                                       bout[:, ei:ei + 1], None, ALU.add))
        C.v.wait_ge(C.sV.h, C.sV.n)
        vm = C.sV.inc(C.v.tensor_copy(
            opi[:, :].rearrange("p (r u) -> p r u", r=16),
            ophi[:, :].rearrange("p (r u) -> p r u", r=16)[:, :, ::-1]))
        C.v.wait_ge(C.sV.h, vm)
        xTr = xT[ei][:, :].rearrange("p (a b c) -> p a b c", a=16, b=8)
        vm = C.sV.inc(C.v.tensor_add(
            xTr, xTr,
            bass.AP(opi.tensor, opi[:, :].offset,
                    [list(opi[:, :].ap)[0], [8, 16], [1, 8], [128, 16]])))
    return (C.sV, C.sV.n)
